# version 09
# baseline (speedup 1.0000x reference)
import numpy as np
import ml_dtypes
import concourse.bass as bass
import concourse.mybir as mybir
from concourse.bass import broadcast_tensor_aps
from concourse.bass_utils import run_bass_kernel_spmd

F32 = mybir.dt.float32
BF16 = mybir.dt.bfloat16
I32 = mybir.dt.int32
AF = mybir.ActivationFunctionType
ALU = mybir.AluOpType
AX = mybir.AxisListType

D = 2048
HID = 8192
T = 2048
TC = 128
NTC = 16
NKD = 16
HC = 512
NHC = 16
SLAB = 2048
NSLAB = 4
NDB = 4
NJB = 4
NIB = 4
CT = 4096
NCC = 32

ENGINES = ("sync", "scalar", "vector", "tensor", "gpsimd")

SEM_NAMES = [
    "cS", "cV", "cT", "cG", "cc",
    "cst", "gbcS", "hboS",
    "w0l0", "w0l1", "w0l2", "w1l0", "w1l1", "w1s0", "w1s1",
    "ax", "ax1", "wld", "wld1", "bias", "bias1", "hst0", "hst1", "hld",
    "hld1", "acc0", "acc1",
    "dacc", "dst", "ex", "est", "jb", "zi0", "zi1", "gst0", "gst1",
    "gld", "ast", "zt0", "zt1", "x2l0", "x2l1", "xst0", "xst1",
]


class Plan:
    def __init__(self):
        self.steps = {e: [] for e in ENGINES}
        self.val = {s: 0 for s in SEM_NAMES}

    def on(self, eng, fn, sem=None, inc=0, waits=()):
        waits = list(waits)
        if eng == "vector" and sem in (None, "cV"):
            waits.append(("cV", self.val["cV"]))
            sem, inc = "cV", 1
        elif eng == "scalar" and sem in (None, "cS"):
            waits.append(("cS", self.val["cS"]))
            sem, inc = "cS", 1
        ws = []
        seen = set()
        for (s, v) in waits:
            if s != "cc":
                assert v <= self.val[s], (eng, s, v, self.val[s])
            if v > 0 and (s, v) not in seen:
                seen.add((s, v))
                ws.append((s, v))
        self.steps[eng].append((tuple(ws), fn, sem, inc))
        if inc:
            self.val[sem] += inc
        return self.val[sem] if sem else 0

    def mark(self, sem):
        return self.val[sem]


def build_program(stages='all'):
    nc = bass.Bass("TRN2", target_bir_lowering=False)

    x_in = nc.dram_tensor("x", [T, D], F32, kind="ExternalInput").ap()
    gw = nc.dram_tensor("gw", [HID, D], F32, kind="ExternalInput").ap()
    vw = nc.dram_tensor("vw", [HID, D], F32, kind="ExternalInput").ap()
    ow = nc.dram_tensor("ow", [D, HID], F32, kind="ExternalInput").ap()
    bg = nc.dram_tensor("bg", [1, HID], F32, kind="ExternalInput").ap()
    bv = nc.dram_tensor("bv", [1, HID], F32, kind="ExternalInput").ap()
    bo = nc.dram_tensor("bo", [1, D], F32, kind="ExternalInput").ap()
    ident_in = nc.dram_tensor("ident", [128, 128], BF16, kind="ExternalInput").ap()
    cst_in = nc.dram_tensor("cst", [128, 8], F32, kind="ExternalInput").ap()
    out_d = nc.dram_tensor("out", [T, D], F32, kind="ExternalOutput").ap()

    tg_t = nc.dram_tensor("tg_t", [D, HID], BF16)
    tv_t = nc.dram_tensor("tv_t", [D, HID], BF16)
    to_t = nc.dram_tensor("to_t", [HID, D], BF16)
    h_s = nc.dram_tensor("h_s", [T, HID], F32)
    x2_s = nc.dram_tensor("x2_s", [T, D], F32)
    x3_s = nc.dram_tensor("x3_s", [T, D], F32)
    acc_s = nc.dram_tensor("acc_s", [T, D], F32)
    gb_d = nc.dram_tensor("gb_d", [1, 8], F32)
    gb3_d = nc.dram_tensor("gb3_d", [3, 1], F32)
    gpart_d = nc.dram_tensor("gpart_d", [128, 3], F32)
    z2hl = nc.dram_tensor("z2hl", [T, 2 * D], BF16)
    z2full = nc.dram_tensor("z2full", [CT, 2 * D], BF16)
    z2t_h = nc.dram_tensor("z2t_h", [D, T], BF16)
    z2t_l = nc.dram_tensor("z2t_l", [D, T], BF16)
    g_s = nc.dram_tensor("g_s", [D, D], F32)
    ah_s = nc.dram_tensor("ah_s", [D, D], BF16)
    al_s = nc.dram_tensor("al_s", [D, D], BF16)

    SB = nc.alloc_sbuf_tensor_at
    SBUF_BASE = 16896
    off = SBUF_BASE

    def place(name, shape, dt_):
        nonlocal off
        t_ = SB(name, shape, dt_, offset=off)
        sz = shape[1] * {F32: 4, BF16: 2, I32: 4}[dt_]
        off += (sz + 31) & ~31
        return t_

    cst = place("cst", [128, 8], F32)
    stats = place("stats", [128, 128], F32)
    gbc = place("gbc", [128, 8], F32)
    gacc = place("gacc", [128, 72], F32)
    gres = place("gres", [1, 8], F32)
    ident = place("ident", [128, 128], BF16)
    assert off <= SBUF_BASE + 4096, off

    off = SBUF_BASE + 4096
    qt = place("qt", [128, 16 * T], BF16)
    wch_base = off
    wch0 = place("wch0", [128, NKD * HC], BF16)
    wch1 = place("wch1", [128, NKD * HC], BF16)
    w1a = SB("w1a", [128, D], F32, offset=wch_base)
    w1b = SB("w1b", [128, D], F32, offset=wch_base + 8192)
    w1c = SB("w1c", [128, D], F32, offset=wch_base + 16384)
    w1qa = SB("w1qa", [128, D], BF16, offset=wch_base + 24576)
    w1qb = SB("w1qb", [128, D], BF16, offset=wch_base + 28672)

    big0 = place("big0", [128, D], F32)
    big1 = place("big1", [128, D], F32)
    big4i = place("big4i", [128, D], I32)
    big5 = place("big5", [128, D], F32)
    big6 = place("big6", [128, D], F32)
    wpfg = SB("wpfg", [128, NKD * HC], BF16, offset=big0.manual_sbuf_range[0]
              if hasattr(big0, "manual_sbuf_range") else 0)
    wpfv = SB("wpfv", [128, NKD * HC], BF16, offset=big4i.manual_sbuf_range[0]
              if hasattr(big4i, "manual_sbuf_range") else 0)
    bigq = place("bigq", [128, D], BF16)
    bigq2 = place("bigq2", [128, D], BF16)

    ev = {}
    for name, width, dt_ in [
        ("gate0", HC, F32), ("gate1", HC, F32), ("gate2a", HC, F32),
        ("gate2b", HC, F32), ("val0", HC, F32), ("val1", HC, F32),
        ("hh0", HC, F32), ("hh1", HC, F32), ("bgb", HC, F32), ("bvb", HC, F32),
        ("bgb1", HC, F32), ("bvb1", HC, F32),
        ("ost0", HC, F32), ("ost1", HC, F32), ("x2a", HC, F32), ("x2b", HC, F32),
    ]:
        ev[name] = place(name, [128, width], dt_)
    hbo = place("hbo", [128, D], F32)
    assert off <= 207 * 1024, off
    ws0 = SB("ws0", [128, 1024], BF16, offset=ev["gate0"].manual_sbuf_range[0])
    ws1 = SB("ws1", [128, 1024], BF16, offset=ev["gate1"].manual_sbuf_range[0])

    qt3 = qt[:, :].rearrange("p (j x) -> p j x", j=16)

    pb = [nc.alloc_psum_tensor(f"pb{i}", [128, 512], F32) for i in range(6)]
    pt = [nc.alloc_psum_tensor(f"pt{i}", [128, 1024], BF16) for i in range(2)]

    p = Plan()

    with nc.Block() as block:
        sems = {s: nc.alloc_semaphore(name="SEM_" + s) for s in SEM_NAMES}

        c_zero = cst[:, 0:1]
        c_one = cst[:, 1:2]
        c_eps = cst[:, 2:3]
        c_mhalf = cst[:, 4:5]

        def stat(col, tc_):
            return stats[:, col + tc_:col + tc_ + 1]

        p.on("sync", lambda e: e.dma_start(out=cst[:, :], in_=cst_in),
             sem="cst", inc=16)
        p.on("sync", lambda e: e.dma_start(out=ident[:, :], in_=ident_in),
             sem="cst", inc=16)

        p.on("vector", lambda e: e.memzero(gacc[:, :]), sem="cV", inc=1)
        pieces = []
        for (src, rows, mi) in [(gw, HID, 0), (vw, HID, 1), (ow, D, 2)]:
            for r0 in range(0, rows, 128):
                for c0 in range(0, src.shape[1], D):
                    pieces.append((src, r0, c0, mi))
        red_marks = {}
        slot_count = [0, 0, 0]
        for k, (src, r0, c0, mi) in enumerate(pieces):
            buf = [w1a, w1b, w1c][k % 3]
            slm = ["w0l0", "w0l1", "w0l2"][k % 3]
            w = [("cV", red_marks[k - 3])] if k >= 3 else []
            p.on("gpsimd", lambda e, s=src, r=r0, c=c0, b=buf:
                 e.dma_start(out=b[:, :], in_=s[r:r + 128, c:c + D]),
                 sem=slm, inc=16, waits=w)
            mdma = p.mark(slm)
            p.on("vector", lambda e, b=buf:
                 e.reduce_sum(gacc[:, 52:68].rearrange("p (a b) -> p a b",
                                                       b=1),
                              b[:, :].rearrange("p (a b) -> p a b", b=128),
                              AX.X, apply_absolute_value=True),
                 waits=[(slm, mdma)])
            p.on("vector", lambda e:
                 e.reduce_sum(gacc[:, 51:52], gacc[:, 52:68], AX.X))
            slot = slot_count[mi] % 16
            slot_count[mi] += 1
            p.on("vector", lambda e, m=mi, sl2=slot:
                 e.tensor_add(gacc[:, m * 16 + sl2:m * 16 + sl2 + 1],
                              gacc[:, m * 16 + sl2:m * 16 + sl2 + 1],
                              gacc[:, 51:52]), sem="cV", inc=1)
            red_marks[k] = p.mark("cV")
        for mi in range(3):
            base = mi * 16
            for lev, width in [(0, 8), (1, 4), (2, 2), (3, 1)]:
                p.on("vector", lambda e, b=base, w2=width:
                     e.tensor_add(gacc[:, b:b + w2],
                                  gacc[:, b:b + w2],
                                  gacc[:, b + w2:b + 2 * w2]))
            p.on("vector", lambda e, b=base, m=mi:
                 e.tensor_copy(gacc[:, 48 + m:49 + m], gacc[:, b:b + 1]))
        p.on("vector", lambda e: e.nop(), sem="cV", inc=1)
        m_w0v = p.mark("cV")
        p.on("sync", lambda e: e.dma_start(out=gpart_d.ap(),
                                           in_=gacc[:, 48:51]),
             sem="gbcS", inc=16, waits=[("cV", m_w0v)])
        p.on("sync", lambda e:
             e.dma_start(out=w1c[0:3, 0:128],
                         in_=gpart_d.ap().rearrange("a b -> b a"),
                         allow_slow_non_contiguous=True),
             sem="gbcS", inc=16, waits=[("gbcS", 16)])
        p.on("vector", lambda e:
             e.reduce_sum(w1c[0:3, 128:144].rearrange("p (a b) -> p a b", b=1),
                          w1c[0:3, 0:128].rearrange("p (a b) -> p a b", b=8),
                          AX.X),
             waits=[("gbcS", 32), ("cV", m_w0v)])
        p.on("vector", lambda e:
             e.reduce_sum(w1c[0:3, 144:145], w1c[0:3, 128:144], AX.X),
             sem="cV", inc=1)
        m_gmm = p.mark("cV")
        p.on("sync", lambda e: e.dma_start(out=gb3_d.ap(),
                                           in_=w1c[0:3, 144:145]),
             sem="gbcS", inc=16, waits=[("cV", m_gmm)])
        p.on("sync", lambda e:
             e.dma_start(out=gres[0:1, 0:3],
                         in_=gb3_d.ap().rearrange("a b -> b a"),
                         allow_slow_non_contiguous=True),
             sem="gbcS", inc=16, waits=[("gbcS", 48)])
        for mi, numel in [(0, HID * D), (1, HID * D), (2, D * HID)]:
            p.on("vector", lambda e, m=mi, n=numel:
                 e.tensor_scalar(gacc[0:1, 69 + m:70 + m], gres[0:1, m:m + 1],
                                 1.0 / n, 1e-5, ALU.mult, ALU.max),
                 waits=[("gbcS", 64)] if mi == 0 else [])
        for mi in range(3):
            p.on("vector", lambda e, m=mi:
                 e.tensor_scalar_mul(gres[0:1, 2 * m:2 * m + 1],
                                     gacc[0:1, 69 + m:70 + m], 0.5))
            p.on("vector", lambda e, m=mi:
                 e.tensor_scalar_mul(gres[0:1, 2 * m + 1:2 * m + 2],
                                     gacc[0:1, 69 + m:70 + m], -0.5))
        for mi, col in [(0, 6), (1, 7)]:
            p.on("vector", lambda e, m=mi, c=col:
                 e.tensor_single_scalar(gres[0:1, c:c + 1],
                                        gres[0:1, 2 * m:2 * m + 1],
                                        2.0 / 127.0, ALU.mult))
        m_gres = p.mark("cV")
        p.on("sync", lambda e: e.dma_start(out=gb_d.ap(), in_=gres[0:1, :]),
             sem="gbcS", inc=16, waits=[("cV", m_gres)])

        def bcast_gb(e):
            i_ap, o_ap = broadcast_tensor_aps(gb_d.ap(), gbc[:, :])
            return e.dma_start(out=o_ap, in_=i_ap)
        p.on("sync", bcast_gb, sem="gbcS", inc=16, waits=[("gbcS", 80)])

        def bcast_bo(e):
            i_ap, o_ap = broadcast_tensor_aps(bo, hbo[:, :])
            return e.dma_start(out=o_ap, in_=i_ap)
        p.on("sync", bcast_bo, sem="hboS", inc=16)
        p.on("vector", lambda e: e.tensor_scalar_mul(hbo[:, :], hbo[:, :], 0.5),
             sem="cV", inc=1, waits=[("hboS", 16)])
        m_hbo = p.mark("cV")

        w1tiles = []
        for hgrp in range(NHC):
            for (src, dst, pcol, ncol) in [(gw, tg_t, 0, 1), (vw, tv_t, 2, 3)]:
                for rr in range(4):
                    w1tiles.append((src, dst, hgrp * 512 + rr * 128, 0,
                                    pcol, ncol))
        for r0 in range(0, D, 128):
            for c0 in range(0, HID, D):
                w1tiles.append((ow, to_t, r0, c0, 4, 5))
        vm_w1 = {}
        trT = {}
        w1ptf = {0: 0, 1: 0}
        for k, (src, dst, r0, c0, pcol, ncol) in enumerate(w1tiles):
            wbuf = [w1a, w1b][k % 2]
            qbuf = [w1qa, w1qb][k % 2]
            ls = "w1l0" if k % 2 == 0 else "w1l1"
            w = [("cV", vm_w1[k - 2])] if k >= 2 else []
            p.on("sync", lambda e, s=src, r=r0, c=c0, b=wbuf:
                 e.dma_start(out=b[:, :], in_=s[r:r + 128, c:c + D]),
                 sem=ls, inc=16, waits=[("gbcS", 96)] + w)
            mld = p.mark(ls)
            w2 = [("cT", trT[k - 2])] if k >= 2 else [("cV", m_w0v)]
            p.on("vector", lambda e, b=wbuf, pc=pcol:
                 e.tensor_single_scalar(w1c[:, :], b[:, :],
                                        gbc[:, pc:pc + 1], ALU.is_ge),
                 waits=[(ls, mld)] + w2)
            p.on("vector", lambda e, b=wbuf, nc_=ncol:
                 e.tensor_single_scalar(b[:, :], b[:, :],
                                        gbc[:, nc_:nc_ + 1], ALU.is_le))
            p.on("vector", lambda e, b=wbuf, q=qbuf:
                 e.tensor_sub(q[:, :], w1c[:, :], b[:, :]), sem="cV", inc=1)
            vm_w1[k] = p.mark("cV")
            for half in range(2):
                ptb = pt[half]
                stg_w = ws0 if half == 0 else ws1
                ssem = "w1s0" if half == 0 else "w1s1"
                for kk in range(8):
                    jc = half * 8 + kk
                    p.on("tensor", lambda e, j=jc, k2=kk, b=ptb, q=qbuf:
                         e.transpose(b[:, k2 * 128:(k2 + 1) * 128],
                                     q[:, j * 128:(j + 1) * 128],
                                     ident[:, :]),
                         sem="cT" if kk == 7 else None,
                         inc=1 if kk == 7 else 0,
                         waits=[("cV", vm_w1[k]), ("cst", 32),
                                ("cS", w1ptf[half])] if kk == 0 else [])
                mtw = p.mark("cT")
                p.on("scalar", lambda e, b=ptb, s=stg_w:
                     e.copy(s[:, :], b[:, :]), sem="cS", inc=1,
                     waits=[("cT", mtw), (ssem, p.mark(ssem))])
                w1ptf[half] = p.mark("cS")

                def st_w(e, d=dst, r=r0, c=c0, h=half, s=stg_w):
                    dview = d.ap()[c + h * 1024:c + (h + 1) * 1024,
                                   r:r + 128]
                    dr = dview.rearrange("(k p) c -> p k c", p=128)
                    sr = s[:, :].rearrange("p (k c) -> p k c", c=128)
                    return e.dma_start(out=dr, in_=sr)
                p.on("gpsimd", st_w, sem=ssem, inc=16,
                     waits=[("cS", w1ptf[half])])
            trT[k] = p.mark("cT")
        m_w1s = [p.mark("w1s0"), p.mark("w1s1")]

        def ffn(src_ap, dst_ap, residual_ap, src_waits, wch_ready_t,
                first_ffn):
            hst_base = [p.mark("hst0"), p.mark("hst1")]
            acc_base = [p.mark("acc0"), p.mark("acc1")]
            pt_free = {0: p.mark("cS"), 1: p.mark("cS")}
            aV = {0: 0, 1: 0}
            aS = {0: 0, 1: 0}
            for tc_ in range(NTC):
                par_a = tc_ % 2
                abuf = big0 if par_a == 0 else big1
                axsem = "ax" if par_a == 0 else "ax1"
                w = list(src_waits) if tc_ < 2 else []
                if aV[par_a]:
                    w.append(("cV", aV[par_a]))
                if aS[par_a]:
                    w.append(("cS", aS[par_a]))
                p.on("sync", lambda e, t=tc_, b=abuf:
                     e.dma_start(out=b[:, :],
                                 in_=src_ap[t * TC:(t + 1) * TC, :]),
                     sem=axsem, inc=16, waits=w)
                mdx = p.mark(axsem)
                p.on("vector", lambda e, b=abuf:
                     e.reduce_sum(stats[:, 114:115], b[:, :], AX.X),
                     waits=[(axsem, mdx)])
                p.on("vector", lambda e, t=tc_:
                     e.tensor_single_scalar(stat(0, t), stats[:, 114:115],
                                            1.0 / D, ALU.mult))
                p.on("scalar", lambda e, b=abuf:
                     e.activation(big6[:, :], b[:, :], AF.Square,
                                  bias=c_zero, scale=c_one,
                                  accum_out=stats[:, 115:116]),
                     sem="cS", inc=1,
                     waits=[(axsem, mdx), ("cst", 32), ("cV", p.mark("cV"))])
                msq = p.mark("cS")
                p.on("vector", lambda e, b=abuf:
                     e.reduce_max(stats[:, 116:117], b[:, :], AX.X))
                p.on("vector", lambda e, b=abuf:
                     e.tensor_reduce(stats[:, 117:118], b[:, :], AX.X,
                                     ALU.min))
                aV[par_a] = p.mark("cV")
                p.on("vector", lambda e:
                     e.tensor_single_scalar(stats[:, 118:119],
                                            stats[:, 115:116], 1.0 / D,
                                            ALU.mult),
                     waits=[("cS", msq)])
                p.on("vector", lambda e, t=tc_:
                     e.tensor_mul(stats[:, 119:120], stat(0, t), stat(0, t)))
                p.on("vector", lambda e:
                     e.tensor_sub(stats[:, 118:119], stats[:, 118:119],
                                  stats[:, 119:120]), sem="cV", inc=1)
                mvv = p.mark("cV")
                p.on("scalar", lambda e:
                     e.activation(stats[:, 119:120], stats[:, 118:119],
                                  AF.Sqrt, bias=c_eps, scale=c_one),
                     sem="cS", inc=1, waits=[("cV", mvv)])
                msr = p.mark("cS")
                p.on("vector", lambda e, t=tc_:
                     e.reciprocal(stat(16, t), stats[:, 119:120]),
                     waits=[("cS", msr)])
                p.on("vector", lambda e, t=tc_:
                     e.tensor_sub(stats[:, 116:117], stats[:, 116:117],
                                  stat(0, t)))
                p.on("vector", lambda e, t=tc_:
                     e.tensor_sub(stats[:, 117:118], stat(0, t),
                                  stats[:, 117:118]))
                p.on("vector", lambda e:
                     e.tensor_max(stats[:, 116:117], stats[:, 116:117],
                                  stats[:, 117:118]))
                p.on("vector", lambda e, t=tc_:
                     e.tensor_mul(stats[:, 116:117], stat(16, t),
                                  stats[:, 116:117]))
                p.on("vector", lambda e:
                     e.tensor_single_scalar(stats[:, 116:117],
                                            stats[:, 116:117], 1e-5,
                                            ALU.max))
                p.on("vector", lambda e, t=tc_:
                     e.tensor_mul(stat(32, t), stats[:, 116:117], gbc[:, 6:7]),
                     waits=[("gbcS", 96)])
                p.on("vector", lambda e, t=tc_:
                     e.tensor_mul(stat(48, t), stats[:, 116:117], gbc[:, 7:8]))
                p.on("vector", lambda e:
                     e.reciprocal(stats[:, 117:118], stats[:, 116:117]))
                p.on("vector", lambda e:
                     e.tensor_single_scalar(stats[:, 117:118],
                                            stats[:, 117:118], 127.0,
                                            ALU.mult))
                p.on("vector", lambda e, t=tc_:
                     e.tensor_mul(stats[:, 117:118], stats[:, 117:118],
                                  stat(16, t)))
                p.on("vector", lambda e, t=tc_:
                     e.tensor_mul(stats[:, 120:121], stat(0, t),
                                  stats[:, 117:118]))
                p.on("vector", lambda e:
                     e.tensor_single_scalar(stats[:, 120:121],
                                            stats[:, 120:121], 0.5, ALU.add))
                p.on("vector", lambda e:
                     e.tensor_single_scalar(stats[:, 120:121],
                                            stats[:, 120:121], -1.0,
                                            ALU.mult), sem="cV", inc=1)
                mbias = p.mark("cV")
                p.on("scalar", lambda e, b=abuf:
                     e.activation(big5[:, :], b[:, :], AF.Identity,
                                  bias=stats[:, 120:121],
                                  scale=stats[:, 117:118]),
                     sem="cS", inc=1, waits=[("cV", mbias)])
                msc5 = p.mark("cS")
                aS[par_a] = msc5
                p.on("vector", lambda e:
                     e.tensor_copy(big4i[:, :], big5[:, :]),
                     waits=[("cS", msc5)])
                p.on("vector", lambda e:
                     e.tensor_tensor(big6[:, :], big5[:, :], big4i[:, :],
                                     ALU.subtract))
                p.on("vector", lambda e:
                     e.scalar_tensor_tensor(bigq[:, :], big6[:, :], 0.0,
                                            big4i[:, :], ALU.is_ge, ALU.add),
                     sem="cV", inc=1, waits=[("cT", p.mark("cT"))])
                mq = p.mark("cV")
                for half in range(2):
                    ptb = pt[half]
                    for kk in range(8):
                        jc = half * 8 + kk
                        p.on("tensor", lambda e, j=jc, k=kk, b=ptb:
                             e.transpose(b[:, k * 128:(k + 1) * 128],
                                         bigq[:, j * 128:(j + 1) * 128],
                                         ident[:, :]),
                             sem="cT" if kk == 7 else None,
                             inc=1 if kk == 7 else 0,
                             waits=([("cV", mq), ("cst", 32),
                                     ("cS", pt_free[half])]
                                    if kk == 0 else []))
                    mt = p.mark("cT")
                    p.on("scalar", lambda e, h=half, t=tc_, b=ptb:
                         e.copy(
                             qt3[:, h * 8:(h + 1) * 8, t * 128:(t + 1) * 128],
                             b[:, :].rearrange("p (k x) -> p k x", k=8)),
                         sem="cS", inc=1, waits=[("cT", mt)])
                    pt_free[half] = p.mark("cS")
            m_phaseA = p.mark("cV")
            m_phaseA_S = p.mark("cS")

            p.on("vector", lambda e: e.memzero(stats[:, 64:80]), sem="cV",
                 inc=1, waits=[("cV", m_phaseA)])
            psum_free = {}
            tens_done = {}
            wld_marks = {}
            bias_marks = {}
            def emit_b_loads(hc_):
                bufg = wch0 if hc_ % 2 == 0 else wpfg
                bufv = wch1 if hc_ % 2 == 0 else wpfv
                wsem = "wld" if hc_ % 2 == 0 else "wld1"
                bsem = "bias" if hc_ % 2 == 0 else "bias1"
                wv = []
                if hc_ >= 2:
                    wv.append(("cT", tens_done[hc_ - 2]))
                elif first_ffn:
                    wv += [("w1s0", m_w1s[0]), ("w1s1", m_w1s[1])]
                else:
                    wv.append(("cT", wch_ready_t))
                if hc_ == 1:
                    wv.append(("cV", p.mark("cV")))
                    wv.append(("cT", p.mark("cT")))
                    wv.append(("cS", p.mark("cS")))
                for (mat_s, b2) in [(tg_t, bufg), (tv_t, bufv)]:
                    for kd in range(NKD):
                        p.on("sync", lambda e, s=mat_s, h=hc_, k=kd, b=b2:
                             e.dma_start(
                                 out=b[:, k * HC:(k + 1) * HC],
                                 in_=s.ap()[k * 128:(k + 1) * 128,
                                            h * HC:(h + 1) * HC]),
                             sem=wsem, inc=16, waits=wv if kd == 0 else [])
                wld_marks[hc_] = (wsem, p.mark(wsem))

                def bc_bias(e, src, dstbuf, h):
                    i_ap, o_ap = broadcast_tensor_aps(
                        src[0:1, h * HC:(h + 1) * HC], dstbuf[:, :])
                    return e.dma_start(out=o_ap, in_=i_ap)
                bgb2 = ev["bgb"] if hc_ % 2 == 0 else ev["bgb1"]
                bvb2 = ev["bvb"] if hc_ % 2 == 0 else ev["bvb1"]
                wb = [("cV", psum_free[(hc_ - 2) * NTC + NTC - 1])]                     if hc_ >= 2 else []
                p.on("sync", lambda e, h=hc_, d2=bgb2: bc_bias(e, bg, d2, h),
                     sem=bsem, inc=16, waits=wb)
                p.on("sync", lambda e, h=hc_, d2=bvb2: bc_bias(e, bv, d2, h),
                     sem=bsem, inc=16)
                bias_marks[hc_] = (bsem, p.mark(bsem))

            emit_b_loads(0)
            for hc_ in range(NHC):
                if hc_ + 1 < NHC:
                    emit_b_loads(hc_ + 1)
                bufg = wch0 if hc_ % 2 == 0 else wpfg
                bufv = wch1 if hc_ % 2 == 0 else wpfv
                wsem, m_wld = wld_marks[hc_]
                bsem, m_bias = bias_marks[hc_]
                bgb2 = ev["bgb"] if hc_ % 2 == 0 else ev["bgb1"]
                bvb2 = ev["bvb"] if hc_ % 2 == 0 else ev["bvb1"]
                for tc_ in range(NTC):
                    ti = hc_ * NTC + tc_
                    par = ti % 2
                    pg, pv = pb[par], pb[2 + par]
                    wv = [("cV", psum_free[ti - 2])] if ti >= 2 else []
                    for kd in range(NKD):
                        p.on("tensor", lambda e, k=kd, t=tc_, g=pg, b=bufg:
                             e.matmul(g[:, 0:HC],
                                      qt[:, k * T + t * 128:
                                         k * T + (t + 1) * 128],
                                      b[:, k * HC:(k + 1) * HC],
                                      start=(k == 0), stop=(k == NKD - 1)),
                             waits=([(wsem, m_wld), ("cS", m_phaseA_S)] + wv)
                             if kd == 0 else [])
                        p.on("tensor", lambda e, k=kd, t=tc_, v=pv, b=bufv:
                             e.matmul(v[:, 0:HC],
                                      qt[:, k * T + t * 128:
                                         k * T + (t + 1) * 128],
                                      b[:, k * HC:(k + 1) * HC],
                                      start=(k == 0), stop=(k == NKD - 1)),
                             sem="cT" if kd == NKD - 1 else None,
                             inc=1 if kd == NKD - 1 else 0)
                    tens_done[hc_] = p.mark("cT")
                    mt = p.mark("cT")
                    gl = ev["gate0"] if par == 0 else ev["gate1"]
                    vl = ev["val0"] if par == 0 else ev["val1"]
                    g2 = ev["gate2a"] if par == 0 else ev["gate2b"]
                    hb = ev["hh0"] if par == 0 else ev["hh1"]
                    hsem = "hst0" if par == 0 else "hst1"
                    wh = [(hsem, hst_base[par] + 16 * (ti // 2))] \
                        if ti >= 2 else []
                    p.on("vector", lambda e, g=pg, t=tc_, o=gl, bb=bgb2:
                         e.scalar_tensor_tensor(o[:, :], g[:, 0:HC],
                                                stat(32, t), bb[:, :],
                                                ALU.mult, ALU.add),
                         waits=[("cT", mt), (bsem, m_bias)] + wh)
                    p.on("vector", lambda e, v=pv, t=tc_, o=vl, bb=bvb2:
                         e.scalar_tensor_tensor(o[:, :], v[:, 0:HC],
                                                stat(48, t), bb[:, :],
                                                ALU.mult, ALU.add),
                         sem="cV", inc=1)
                    psum_free[ti] = p.mark("cV")
                    mev = p.mark("cV")
                    p.on("scalar", lambda e, i=gl, o=g2:
                         e.activation(o[:, :], i[:, :], AF.Silu,
                                      bias=c_zero, scale=c_one),
                         sem="cS", inc=1, waits=[("cV", mev)])
                    msil = p.mark("cS")
                    p.on("vector", lambda e, a=g2, b=vl, o=hb:
                         e.tensor_mul(o[:, :], a[:, :], b[:, :]),
                         waits=[("cS", msil)])
                    p.on("vector", lambda e, o=hb:
                         e.reduce_max(stats[:, 118:119], o[:, :], AX.X,
                                      apply_absolute_value=True))
                    p.on("vector", lambda e, t=tc_:
                         e.tensor_max(stat(64, t), stat(64, t),
                                      stats[:, 118:119]), sem="cV", inc=1)
                    mh = p.mark("cV")
                    p.on("gpsimd", lambda e, t=tc_, h=hc_, o=hb:
                         e.dma_start(out=h_s.ap()[t * TC:(t + 1) * TC,
                                                  h * HC:(h + 1) * HC],
                                     in_=o[:, :]),
                         sem=hsem, inc=16, waits=[("cV", mh)])
            m_hst = [p.mark("hst0"), p.mark("hst1")]

            p.on("vector", lambda e:
                 e.tensor_single_scalar(stats[:, 64:80], stats[:, 64:80],
                                        1e-5, ALU.max))
            p.on("vector", lambda e:
                 e.tensor_single_scalar(stats[:, 80:96], stats[:, 64:80],
                                        gbc[:, 4:5], ALU.mult))
            p.on("vector", lambda e:
                 e.tensor_single_scalar(stats[:, 80:96], stats[:, 80:96],
                                        1.0 / 127.0, ALU.mult))
            p.on("vector", lambda e:
                 e.reciprocal(stats[:, 96:112], stats[:, 64:80]))
            p.on("vector", lambda e:
                 e.tensor_single_scalar(stats[:, 96:112], stats[:, 96:112],
                                        127.0, ALU.mult), sem="cV", inc=1)
            m_gh = p.mark("cV")

            ostages = [ev["ost0"], ev["ost1"]]
            sc_free = {}
            qev = {}
            to_marks = {}
            tdone_c = {}
            hfree = {0: 0, 1: 0}
            bigq_free = [p.mark("cT")]

            def emit_c_toload(sl, db):
                gi = sl * NDB + db
                buf = wch0 if gi % 2 == 0 else wch1
                wsem = "wld" if gi % 2 == 0 else "wld1"
                wv = []
                if gi >= 2:
                    wv.append(("cT", tdone_c[gi - 2]))
                else:
                    wv.append(("cT", p.mark("cT")))
                    if first_ffn:
                        wv += [("w1s0", m_w1s[0]), ("w1s1", m_w1s[1])]
                for kd in range(NKD):
                    p.on("scalar", lambda e, s2=sl, d=db, k=kd, b=buf:
                         e.dma_start(
                             out=b[:, k * HC:(k + 1) * HC],
                             in_=to_t.ap()[s2 * SLAB + k * 128:
                                           s2 * SLAB + (k + 1) * 128,
                                           d * 512:(d + 1) * 512]),
                         sem=wsem, inc=16, waits=wv if kd == 0 else [])
                to_marks[gi] = (wsem, p.mark(wsem))

            def emit_c_quant(sl, tc_):
                par_h = tc_ % 2
                hbuf = big0 if par_h == 0 else big1
                hsem = "hld" if par_h == 0 else "hld1"
                w = [("hst0", m_hst[0]), ("hst1", m_hst[1]), ("cV", m_gh)]
                if hfree[par_h]:
                    w.append(("cS", hfree[par_h]))
                p.on("sync", lambda e, s=sl, t=tc_, b=hbuf:
                     e.dma_start(out=b[:, :],
                                 in_=h_s.ap()[t * TC:(t + 1) * TC,
                                              s * SLAB:(s + 1) * SLAB]),
                     sem=hsem, inc=16, waits=w)
                mdh = p.mark(hsem)
                p.on("scalar", lambda e, t=tc_, b=hbuf:
                     e.activation(big5[:, :], b[:, :], AF.Identity,
                                  bias=c_mhalf, scale=stat(96, t)),
                     sem="cS", inc=1,
                     waits=[(hsem, mdh), ("cV", max(m_gh, p.mark("cV")))])
                msc = p.mark("cS")
                hfree[par_h] = msc
                p.on("vector", lambda e:
                     e.tensor_copy(big4i[:, :], big5[:, :]),
                     waits=[("cS", msc)])
                p.on("vector", lambda e:
                     e.tensor_tensor(big6[:, :], big5[:, :], big4i[:, :],
                                     ALU.subtract))
                p.on("vector", lambda e:
                     e.scalar_tensor_tensor(bigq[:, :], big6[:, :], 0.0,
                                            big4i[:, :], ALU.is_ge,
                                            ALU.add),
                     sem="cV", inc=1, waits=[("cT", bigq_free[0])])
                mq = p.mark("cV")
                for half in range(2):
                    ptb = pt[half]
                    for kk in range(8):
                        jc = half * 8 + kk
                        p.on("tensor", lambda e, j=jc, k=kk, b=ptb:
                             e.transpose(b[:, k * 128:(k + 1) * 128],
                                         bigq[:, j * 128:(j + 1) * 128],
                                         ident[:, :]),
                             sem="cT" if kk == 7 else None,
                             inc=1 if kk == 7 else 0,
                             waits=[("cV", mq),
                                    ("cS", pt_free[half])]
                             if kk == 0 else [])
                    mt = p.mark("cT")
                    p.on("scalar", lambda e, h=half, t=tc_, b=ptb:
                         e.copy(
                             qt3[:, h * 8:(h + 1) * 8,
                                 t * 128:(t + 1) * 128],
                             b[:, :].rearrange("p (k x) -> p k x", k=8)),
                         sem="cS", inc=1, waits=[("cT", mt)])
                    pt_free[half] = p.mark("cS")
                bigq_free[0] = p.mark("cT")
                qev[(sl, tc_)] = p.mark("cS")

            nacc = 0
            for t0 in range(NTC):
                emit_c_quant(0, t0)
            for sl in range(NSLAB):
                macc_prev = (p.mark("acc0"), p.mark("acc1"))
                accop = ALU.bypass if sl == 0 else ALU.add
                if sl == 0:
                    emit_c_toload(0, 0)
                for db in range(NDB):
                    gi = sl * NDB + db
                    if gi + 1 < NSLAB * NDB:
                        emit_c_toload((gi + 1) // NDB, (gi + 1) % NDB)
                    wsem_c, m_told = to_marks[gi]
                    bufc = wch0 if gi % 2 == 0 else wch1
                    for tc_ in range(NTC):
                        ti2 = db * NTC + tc_
                        par = ti2 % 2
                        po = pb[4] if par == 0 else pb[5]
                        wv2 = [("cS", sc_free[(sl, ti2 - 2)])] \
                            if ti2 >= 2 else []
                        for kd in range(NKD):
                            p.on("tensor", lambda e, k=kd, t=tc_, o=po,
                                 b=bufc:
                                 e.matmul(o[:, 0:512],
                                          qt[:, k * T + t * 128:
                                             k * T + (t + 1) * 128],
                                          b[:, k * HC:(k + 1) * HC],
                                          start=(k == 0), stop=(k == NKD - 1)),
                                 sem="cT" if kd == NKD - 1 else None,
                                 inc=1 if kd == NKD - 1 else 0,
                                 waits=([(wsem_c, m_told),
                                         ("cS", qev[(sl, tc_)])] + wv2)
                                 if kd == 0 else [])
                        mt = p.mark("cT")
                        tdone_c[gi] = mt
                        ost = ostages[par]
                        asem = "acc0" if par == 0 else "acc1"
                        wc = [(asem, acc_base[par] + 16 * (nacc // 2))] \
                            if nacc >= 2 else []
                        nacc += 1
                        p.on("scalar", lambda e, o=po, st=ost:
                             e.copy(st[:, :], o[:, 0:512]),
                             sem="cS", inc=1, waits=[("cT", mt)] + wc)
                        sc_free[(sl, ti2)] = p.mark("cS")
                        mcp = p.mark("cS")
                        wacc = [("cS", mcp)]
                        if sl > 0 and db == 0 and tc_ == 0:
                            wacc += [("acc0", macc_prev[0]),
                                     ("acc1", macc_prev[1])]
                        p.on("gpsimd", lambda e, t=tc_, d=db, st=ost, a=accop:
                             e.dma_start(out=acc_s.ap()[t * TC:(t + 1) * TC,
                                                        d * 512:(d + 1) * 512],
                                         in_=st[:, :], accum_op=a),
                             sem=asem, inc=16, waits=wacc)
                        if db == NDB - 1 and sl + 1 < NSLAB:
                            emit_c_quant(sl + 1, tc_)
            m_acc = [p.mark("acc0"), p.mark("acc1")]

            for tc_ in range(NTC):
                w = [("acc0", m_acc[0]), ("acc1", m_acc[1])]
                if tc_ >= 1:
                    w.append(("cV", p.mark("cV")))
                    w.append(("dst", p.mark("dst")))
                p.on("sync", lambda e, t=tc_:
                     e.dma_start(out=big0[:, :],
                                 in_=acc_s.ap()[t * TC:(t + 1) * TC, :]),
                     sem="dacc", inc=16, waits=w)
                p.on("sync", lambda e, t=tc_:
                     e.dma_start(out=big1[:, :],
                                 in_=residual_ap[t * TC:(t + 1) * TC, :]),
                     sem="dacc", inc=16)
                mdd = p.mark("dacc")
                p.on("vector", lambda e, t=tc_:
                     e.scalar_tensor_tensor(big6[:, :], big0[:, :],
                                            stat(80, t), hbo[:, :],
                                            ALU.mult, ALU.add),
                     waits=[("dacc", mdd), ("cV", m_hbo)])
                p.on("vector", lambda e:
                     e.tensor_add(big0[:, :], big6[:, :], big1[:, :]),
                     sem="cV", inc=1)
                mvd = p.mark("cV")
                p.on("gpsimd", lambda e, t=tc_:
                     e.dma_start(out=dst_ap[t * TC:(t + 1) * TC, :],
                                 in_=big0[:, :]),
                     sem="dst", inc=16, waits=[("cV", mvd)])
            return p.mark("dst")

        if stages == "ffn1":
            m_dst = ffn(x_in, out_d, x_in, [], 0, True)
            p.on("gpsimd", lambda e: e.wait_ge(sems["dst"], m_dst))
            def emit(eng_name, engine):
                for waits, fn, sem, inc in p.steps[eng_name]:
                    for (s2, v) in waits:
                        engine.wait_ge(sems[s2], v)
                    inst = fn(engine)
                    if inc and inst is not None:
                        inst.then_inc(sems[sem], inc)
            @block.sync
            def _(sync):
                emit("sync", sync)
            @block.scalar
            def _(scalar):
                emit("scalar", scalar)
            @block.vector
            def _(vector):
                emit("vector", vector)
            @block.tensor
            def _(tensor):
                emit("tensor", tensor)
            @block.gpsimd
            def _(gpsimd):
                emit("gpsimd", gpsimd)
            return nc
        m_ffn1_dst = ffn(x_in, x2_s.ap(), x_in, [], 0, True)

        for tc_ in range(NTC):
            w = [("dst", m_ffn1_dst)]
            if tc_ >= 1:
                w += [("cV", p.mark("cV")), ("cS", p.mark("cS"))]
            p.on("sync", lambda e, t=tc_:
                 e.dma_start(out=big0[:, :],
                             in_=x2_s.ap()[t * TC:(t + 1) * TC, :]),
                 sem="ex", inc=16, waits=w)
            mdx = p.mark("ex")
            p.on("vector", lambda e:
                 e.reduce_sum(stats[:, 114:115], big0[:, :], AX.X),
                 waits=[("ex", mdx)])
            p.on("vector", lambda e:
                 e.tensor_single_scalar(stats[:, 114:115], stats[:, 114:115],
                                        1.0 / D, ALU.mult))
            p.on("scalar", lambda e:
                 e.activation(big1[:, :], big0[:, :], AF.Square,
                              bias=c_zero, scale=c_one,
                              accum_out=stats[:, 115:116]),
                 sem="cS", inc=1, waits=[("ex", mdx), ("cV", p.mark("cV"))])
            msq = p.mark("cS")
            p.on("vector", lambda e:
                 e.tensor_single_scalar(stats[:, 118:119], stats[:, 115:116],
                                        1.0 / D, ALU.mult),
                 waits=[("cS", msq)])
            p.on("vector", lambda e:
                 e.tensor_mul(stats[:, 119:120], stats[:, 114:115],
                              stats[:, 114:115]))
            p.on("vector", lambda e:
                 e.tensor_sub(stats[:, 118:119], stats[:, 118:119],
                              stats[:, 119:120]), sem="cV", inc=1)
            mvv = p.mark("cV")
            p.on("scalar", lambda e:
                 e.activation(stats[:, 119:120], stats[:, 118:119], AF.Sqrt,
                              bias=c_eps, scale=c_one),
                 sem="cS", inc=1, waits=[("cV", mvv)])
            msr = p.mark("cS")
            p.on("vector", lambda e:
                 e.reciprocal(stats[:, 118:119], stats[:, 119:120]),
                 waits=[("cS", msr)])
            p.on("vector", lambda e:
                 e.tensor_scalar(big5[:, :], big0[:, :], stats[:, 114:115],
                                 stats[:, 118:119], ALU.subtract, ALU.mult),
                 waits=[("cS", p.mark("cS"))])
            p.on("scalar", lambda e: e.copy(bigq[:, :], big5[:, :]),
                 sem="cS", inc=1,
                 waits=[("cT", p.mark("cT")), ("est", p.mark("est")),
                        ("cV", p.mark("cV"))])
            mzh = p.mark("cS")
            p.on("vector", lambda e:
                 e.tensor_tensor(bigq2[:, :], big5[:, :], bigq[:, :],
                                 ALU.subtract), sem="cV", inc=1,
                 waits=[("cS", mzh)])
            mz = p.mark("cV")
            p.on("gpsimd", lambda e, t=tc_:
                 e.dma_start(out=z2hl.ap()[t * TC:(t + 1) * TC, 0:D],
                             in_=bigq[:, :]), sem="est", inc=16,
                 waits=[("cS", mzh)])
            p.on("gpsimd", lambda e, t=tc_:
                 e.dma_start(out=z2hl.ap()[t * TC:(t + 1) * TC, D:2 * D],
                             in_=bigq2[:, :]), sem="est", inc=16,
                 waits=[("cV", mz)])
            for (qbuf, dstt) in [(bigq, z2t_h), (bigq2, z2t_l)]:
                for half in range(2):
                    ptb = pt[half]
                    for kk in range(8):
                        jc = half * 8 + kk
                        p.on("tensor", lambda e, j=jc, k=kk, b=ptb, q=qbuf:
                             e.transpose(b[:, k * 128:(k + 1) * 128],
                                         q[:, j * 128:(j + 1) * 128],
                                         ident[:, :]),
                             sem="cT" if kk == 7 else None,
                             inc=1 if kk == 7 else 0,
                             waits=[("cV", max(mz, p.mark("cV"))),
                                    ("cS", p.mark("cS"))]
                             if kk == 0 else [])
                    mtt = p.mark("cT")
                    stg = wch0 if half == 0 else wch1
                    if half == 0:
                        p.on("scalar", lambda e, b=ptb, s=stg:
                             e.copy(s[:, 0:1024], b[:, :]),
                             sem="cS", inc=1,
                             waits=[("cT", mtt), ("est", p.mark("est"))])
                        mvt = ("cS", p.mark("cS"))
                    else:
                        p.on("vector", lambda e, b=ptb, s=stg:
                             e.tensor_copy(s[:, 0:1024], b[:, :]),
                             sem="cV", inc=1,
                             waits=[("cT", mtt), ("est", p.mark("est"))])
                        mvt = ("cV", p.mark("cV"))

                    def st_t(e, t=tc_, h=half, s=stg, d=dstt):
                        dview = d.ap()[h * 1024:(h + 1) * 1024,
                                       t * 128:(t + 1) * 128]
                        dr = dview.rearrange("(k p) c -> p k c", p=128)
                        sr = s[:, 0:1024].rearrange("p (k c) -> p k c", c=128)
                        return e.dma_start(out=dr, in_=sr)
                    p.on("gpsimd", st_t, sem="est", inc=16,
                         waits=[mvt])
        m_est = p.mark("est")

        def do_ag(e):
            e.collective_compute(
                "AllGather", ALU.bypass,
                replica_groups=[[0, 1], [2, 3], [4, 5], [6, 7]],
                ins=[z2hl.ap().opt()],
                outs=[z2full.ap().opt()]).then_inc(sems["cc"], 1)
            return None
        p.on("gpsimd", do_ag, waits=[("est", m_est)])

        for jbi in range(NJB):
            wv = [("cT", p.mark("cT"))]
            for (hl, base) in [(0, 0), (1, D)]:
                for cc2 in range(NCC):
                    p.on("sync", lambda e, j=jbi, c=cc2, b=base, o=hl:
                         e.dma_start(
                             out=qt[:, (o * NCC + c) * 512:
                                    (o * NCC + c + 1) * 512],
                             in_=z2full.ap()[c * 128:(c + 1) * 128,
                                             b + j * 512:b + (j + 1) * 512]),
                         sem="jb", inc=16,
                         waits=(wv + [("cc", 1)])
                         if (cc2 == 0 and hl == 0) else [])
            m_jb = p.mark("jb")
            for ic in range(16):
                ti = jbi * 16 + ic
                ibh = wch0 if ic % 2 == 0 else wch1
                zsem = "zi0" if ic % 2 == 0 else "zi1"
                wv2 = [("cT", p.mark("cT"))]
                for (hl, base) in [(0, 0), (1, D)]:
                    def ld_i(e, i=ic, b=base, buf=ibh, o=hl):
                        src = z2full.ap()[0:CT, b + i * 128:b + (i + 1) * 128]
                        sr = src.rearrange("(c p) i -> p c i", p=128)
                        dst2 = buf[:, o * NCC * 128:(o + 1) * NCC * 128]
                        dr = dst2.rearrange("p (c i) -> p c i", i=128)
                        return e.dma_start(out=dr, in_=sr)
                    p.on("sync", ld_i, sem=zsem, inc=16,
                         waits=([("jb", m_jb)] + wv2) if hl == 0 else [])
                m_ic = p.mark(zsem)
                par = ti % 2
                pg = pb[par]
                gsem = "gst0" if par == 0 else "gst1"
                wv3 = [("cV", p.mark("cV"))] if ti >= 2 else []
                passes = [(c3, la, ra) for c3 in range(NCC)
                          for (la, ra) in [(0, 0), (0, 1), (1, 0)]]
                for idx, (c3, la, ra) in enumerate(passes):
                    last = idx == len(passes) - 1
                    p.on("tensor", lambda e, lo=(la * NCC + c3) * 128,
                         ro=(ra * NCC + c3) * 512, g=pg, b=ibh,
                         st=(idx == 0), sp=last:
                         e.matmul(g[:, 0:512], b[:, lo:lo + 128],
                                  qt[:, ro:ro + 512], start=st, stop=sp),
                         sem="cT" if last else None, inc=1 if last else 0,
                         waits=([(zsem, m_ic), ("jb", m_jb)] + wv3)
                         if idx == 0 else [])
                mt = p.mark("cT")
                stg = ev["ost0"] if par == 0 else ev["ost1"]
                wg = [(gsem, 16 * (ti // 2))] if ti >= 2 else []
                p.on("vector", lambda e, g=pg, s=stg:
                     e.tensor_copy(s[:, :], g[:, 0:512]), sem="cV", inc=1,
                     waits=[("cT", mt)] + wg)
                mve = p.mark("cV")
                p.on("gpsimd", lambda e, i=ic, j=jbi, s=stg:
                     e.dma_start(out=g_s.ap()[i * 128:(i + 1) * 128,
                                              j * 512:(j + 1) * 512],
                                 in_=s[:, :]),
                     sem=gsem, inc=16, waits=[("cV", mve)])
        m_gst = [p.mark("gst0"), p.mark("gst1")]
        m_gram_t = p.mark("cT")

        for ic in range(16):
            w = [("gst0", m_gst[0]), ("gst1", m_gst[1])]
            if ic >= 1:
                w += [("cV", p.mark("cV")), ("cS", p.mark("cS"))]
            p.on("sync", lambda e, i=ic:
                 e.dma_start(out=big0[:, :],
                             in_=g_s.ap()[i * 128:(i + 1) * 128, :]),
                 sem="gld", inc=16, waits=w)
            mdg = p.mark("gld")
            p.on("vector", lambda e:
                 e.reduce_max(stats[:, 114:115], big0[:, :], AX.X),
                 waits=[("gld", mdg)])
            p.on("vector", lambda e:
                 e.tensor_single_scalar(stats[:, 115:116], stats[:, 114:115],
                                        -1.0, ALU.mult), sem="cV", inc=1)
            mvm = p.mark("cV")
            p.on("scalar", lambda e:
                 e.activation(big1[:, :], big0[:, :], AF.Exp,
                              bias=stats[:, 115:116], scale=c_one,
                              accum_out=stats[:, 116:117]),
                 sem="cS", inc=1, waits=[("cV", mvm)])
            mse = p.mark("cS")
            p.on("vector", lambda e:
                 e.reciprocal(stats[:, 117:118], stats[:, 116:117]),
                 waits=[("cS", mse)])
            p.on("vector", lambda e:
                 e.tensor_single_scalar(big6[:, :], big1[:, :],
                                        stats[:, 117:118], ALU.mult),
                 waits=[("cS", p.mark("cS"))])
            p.on("scalar", lambda e: e.copy(bigq[:, :], big6[:, :]),
                 sem="cS", inc=1,
                 waits=[("ast", p.mark("ast")), ("cV", p.mark("cV"))])
            mha = p.mark("cS")
            p.on("vector", lambda e:
                 e.tensor_tensor(bigq2[:, :], big6[:, :], bigq[:, :],
                                 ALU.subtract), sem="cV", inc=1,
                 waits=[("cS", mha)])
            mva = p.mark("cV")
            p.on("gpsimd", lambda e, i=ic:
                 e.dma_start(out=ah_s.ap()[i * 128:(i + 1) * 128, :],
                             in_=bigq[:, :]), sem="ast", inc=16,
                 waits=[("cS", mha)])
            p.on("gpsimd", lambda e, i=ic:
                 e.dma_start(out=al_s.ap()[i * 128:(i + 1) * 128, :],
                             in_=bigq2[:, :]), sem="ast", inc=16,
                 waits=[("cV", mva)])
        m_ast = p.mark("ast")

        ctx_add_mark = {}
        for ib in range(NIB):
            wv = [("cT", p.mark("cT"))]
            for (hl, src) in [(0, ah_s), (1, al_s)]:
                for jc in range(16):
                    p.on("scalar", lambda e, i=ib, j=jc, s=src, o=hl:
                         e.dma_start(
                             out=qt[:, (o * 16 + j) * 512:
                                    (o * 16 + j + 1) * 512],
                             in_=s.ap()[i * 512:(i + 1) * 512,
                                        j * 128:(j + 1) * 128],
                             transpose=True),
                         sem="wld", inc=16,
                         waits=([("ast", m_ast)] + wv)
                         if (jc == 0 and hl == 0) else [])
            m_at = p.mark("wld")
            for cc2 in range(NTC):
                ti = ib * NTC + cc2
                parb = cc2 % 2
                zb = wch0 if parb == 0 else wch1
                ztsem = "zt0" if parb == 0 else "zt1"
                xsem = "x2l0" if parb == 0 else "x2l1"
                wv2 = [("cT", p.mark("cT"))] if ti >= 2 else \
                    [("cT", m_gram_t)]
                for (hl, src) in [(0, z2t_h), (1, z2t_l)]:
                    def ld_zt(e, c=cc2, s=src, o=hl, buf=zb):
                        sv = s.ap()[0:D, c * 128:(c + 1) * 128]
                        sr = sv.rearrange("(j p) c -> p j c", p=128)
                        dst2 = buf[:, o * 16 * 128:(o + 1) * 16 * 128]
                        dr = dst2.rearrange("p (j c) -> p j c", c=128)
                        return e.dma_start(out=dr, in_=sr)
                    p.on("sync", ld_zt, sem=ztsem, inc=16,
                         waits=([("est", m_est)] + wv2) if hl == 0 else [])
                m_zt = p.mark(ztsem)
                x2t = ev["x2a"] if parb == 0 else ev["x2b"]
                wx = [("cV", ctx_add_mark[ti - 2])] if ti >= 2 else []
                p.on("sync", lambda e, c=cc2, i=ib, o=x2t:
                     e.dma_start(out=o[:, :],
                                 in_=x2_s.ap()[c * 128:(c + 1) * 128,
                                               i * 512:(i + 1) * 512]),
                     sem=xsem, inc=16, waits=wx)
                m_x2 = p.mark(xsem)
                par = ti % 2
                po = pb[2 + par]
                xssem = "xst0" if par == 0 else "xst1"
                wv3 = [("cV", p.mark("cV"))] if ti >= 2 else []
                passes = [(jc, la, ra) for jc in range(16)
                          for (la, ra) in [(0, 0), (0, 1), (1, 0)]]
                for idx, (jc, la, ra) in enumerate(passes):
                    last = idx == len(passes) - 1
                    p.on("tensor", lambda e, lo=(la * 16 + jc) * 128,
                         ro=(ra * 16 + jc) * 512, o=po, b=zb,
                         st=(idx == 0), sp=last:
                         e.matmul(o[:, 0:512], b[:, lo:lo + 128],
                                  qt[:, ro:ro + 512], start=st, stop=sp),
                         sem="cT" if last else None, inc=1 if last else 0,
                         waits=([(ztsem, m_zt), ("wld", m_at)] + wv3)
                         if idx == 0 else [])
                mt = p.mark("cT")
                stg = ev["ost0"] if par == 0 else ev["ost1"]
                wst = [(xssem, 16 * (ti // 2))] if ti >= 2 else []
                p.on("vector", lambda e, o=po, s=stg, xx=x2t:
                     e.tensor_add(s[:, :], o[:, 0:512], xx[:, :]),
                     sem="cV", inc=1,
                     waits=[("cT", mt), (xsem, m_x2)] + wst)
                ctx_add_mark[ti] = p.mark("cV")
                mve = p.mark("cV")
                p.on("gpsimd", lambda e, c=cc2, i=ib, s=stg:
                     e.dma_start(out=x3_s.ap()[c * 128:(c + 1) * 128,
                                               i * 512:(i + 1) * 512],
                                 in_=s[:, :]),
                     sem=xssem, inc=16, waits=[("cV", mve)])
        m_xst = [p.mark("xst0"), p.mark("xst1")]
        m_ctx_t = p.mark("cT")

        m_ffn3_dst = ffn(x3_s.ap(), out_d, x3_s.ap(),
                         [("xst0", m_xst[0]), ("xst1", m_xst[1])],
                         m_ctx_t, False)

        p.on("gpsimd", lambda e: e.wait_ge(sems["dst"], m_ffn3_dst))

        def emit(eng_name, engine):
            for waits, fn, sem, inc in p.steps[eng_name]:
                for (s2, v) in waits:
                    engine.wait_ge(sems[s2], v)
                inst = fn(engine)
                if inc and inst is not None:
                    inst.then_inc(sems[sem], inc)

        @block.sync
        def _(sync):
            emit("sync", sync)

        @block.scalar
        def _(scalar):
            emit("scalar", scalar)

        @block.vector
        def _(vector):
            emit("vector", vector)

        @block.tensor
        def _(tensor):
            emit("tensor", tensor)

        @block.gpsimd
        def _(gpsimd):
            emit("gpsimd", gpsimd)

    return nc


_NC_CACHE = None


def kernel(L0_attn_x, L0_ffn_gate_w, L0_ffn_gate_b, L0_ffn_val_w,
           L0_ffn_val_b, L0_ffn_out_w, L0_ffn_out_b):
    global _NC_CACHE
    x = np.ascontiguousarray(np.asarray(L0_attn_x, dtype=np.float32))
    gw_a = np.ascontiguousarray(np.asarray(L0_ffn_gate_w, dtype=np.float32))
    vw_a = np.ascontiguousarray(np.asarray(L0_ffn_val_w, dtype=np.float32))
    ow_a = np.ascontiguousarray(np.asarray(L0_ffn_out_w, dtype=np.float32))
    bg_a = np.ascontiguousarray(
        np.asarray(L0_ffn_gate_b, dtype=np.float32).reshape(1, HID))
    bv_a = np.ascontiguousarray(
        np.asarray(L0_ffn_val_b, dtype=np.float32).reshape(1, HID))
    bo_a = np.ascontiguousarray(
        np.asarray(L0_ffn_out_b, dtype=np.float32).reshape(1, D))
    ident = np.eye(128, dtype=ml_dtypes.bfloat16)
    cstrow = np.array([0.0, 1.0, 1e-5, 0.5, -0.5, 1.0 / D, 127.0, 0.0],
                      np.float32)
    cst_a = np.ascontiguousarray(np.broadcast_to(cstrow, (128, 8)))

    if _NC_CACHE is None:
        _NC_CACHE = build_program()
    nc = _NC_CACHE

    in_maps = []
    for c in range(8):
        b, hh = divmod(c, 2)
        in_maps.append({
            "x": np.ascontiguousarray(x[b, hh * T:(hh + 1) * T, :]),
            "gw": gw_a, "vw": vw_a, "ow": ow_a,
            "bg": bg_a, "bv": bv_a, "bo": bo_a,
            "ident": ident, "cst": cst_a,
        })
    res = run_bass_kernel_spmd(nc, in_maps, core_ids=list(range(8)))
    out = np.empty((4, 4096, D), np.float32)
    for c in range(8):
        b, hh = divmod(c, 2)
        out[b, hh * T:(hh + 1) * T, :] = np.asarray(res.results[c]["out"])
    return out


# version 10 (final)
# speedup vs baseline: 1.0941x; 1.0941x over previous
import numpy as np
import ml_dtypes
import concourse.bass as bass
import concourse.mybir as mybir
from concourse.bass import broadcast_tensor_aps
from concourse.bass_utils import run_bass_kernel_spmd

F32 = mybir.dt.float32
BF16 = mybir.dt.bfloat16
I32 = mybir.dt.int32
AF = mybir.ActivationFunctionType
ALU = mybir.AluOpType
AX = mybir.AxisListType

D = 2048
HID = 8192
T = 2048
TC = 128
NTC = 16
NKD = 16
HC = 512
NHC = 16
SLAB = 2048
NSLAB = 4
NDB = 4
NJB = 4
NIB = 4
CT = 4096
NCC = 32

ENGINES = ("sync", "scalar", "vector", "tensor", "gpsimd")

SEM_NAMES = [
    "cS", "cV", "cT", "cG", "cc",
    "cst", "gbcS", "hboS",
    "w0l0", "w0l1", "w0l2", "w1l0", "w1l1", "w1s0", "w1s1",
    "ax", "ax1", "wld", "wld1", "bias", "bias1", "hst0", "hst1", "hld",
    "hld1", "acc0", "acc1",
    "dacc", "dst", "ex", "est", "jb", "zi0", "zi1", "gst0", "gst1",
    "gld", "ast", "zt0", "zt1", "x2l0", "x2l1", "xst0", "xst1",
]


class Plan:
    def __init__(self):
        self.steps = {e: [] for e in ENGINES}
        self.val = {s: 0 for s in SEM_NAMES}

    def on(self, eng, fn, sem=None, inc=0, waits=()):
        waits = list(waits)
        if eng == "vector" and sem in (None, "cV"):
            waits.append(("cV", self.val["cV"]))
            sem, inc = "cV", 1
        elif eng == "scalar" and sem in (None, "cS"):
            waits.append(("cS", self.val["cS"]))
            sem, inc = "cS", 1
        ws = []
        seen = set()
        for (s, v) in waits:
            if s != "cc":
                assert v <= self.val[s], (eng, s, v, self.val[s])
            if v > 0 and (s, v) not in seen:
                seen.add((s, v))
                ws.append((s, v))
        self.steps[eng].append((tuple(ws), fn, sem, inc))
        if inc:
            self.val[sem] += inc
        return self.val[sem] if sem else 0

    def mark(self, sem):
        return self.val[sem]


def build_program(stages='all'):
    nc = bass.Bass("TRN2", target_bir_lowering=False)

    x_in = nc.dram_tensor("x", [T, D], F32, kind="ExternalInput").ap()
    gw = nc.dram_tensor("gw", [HID, D], F32, kind="ExternalInput").ap()
    vw = nc.dram_tensor("vw", [HID, D], F32, kind="ExternalInput").ap()
    ow = nc.dram_tensor("ow", [D, HID], F32, kind="ExternalInput").ap()
    bg = nc.dram_tensor("bg", [1, HID], F32, kind="ExternalInput").ap()
    bv = nc.dram_tensor("bv", [1, HID], F32, kind="ExternalInput").ap()
    bo = nc.dram_tensor("bo", [1, D], F32, kind="ExternalInput").ap()
    ident_in = nc.dram_tensor("ident", [128, 128], BF16, kind="ExternalInput").ap()
    cst_in = nc.dram_tensor("cst", [128, 8], F32, kind="ExternalInput").ap()
    out_d = nc.dram_tensor("out", [T, D], F32, kind="ExternalOutput").ap()

    tg_t = nc.dram_tensor("tg_t", [D, HID], BF16)
    tv_t = nc.dram_tensor("tv_t", [D, HID], BF16)
    to_t = nc.dram_tensor("to_t", [HID, D], BF16)
    h_s = nc.dram_tensor("h_s", [T, HID], F32)
    x2_s = nc.dram_tensor("x2_s", [T, D], F32)
    x3_s = nc.dram_tensor("x3_s", [T, D], F32)
    acc_s = nc.dram_tensor("acc_s", [T, D], F32)
    gb_d = nc.dram_tensor("gb_d", [1, 8], F32)
    gb3_d = nc.dram_tensor("gb3_d", [3, 1], F32)
    gpart_d = nc.dram_tensor("gpart_d", [128, 3], F32)
    z2hl = nc.dram_tensor("z2hl", [T, 2 * D], BF16)
    z2full = nc.dram_tensor("z2full", [CT, 2 * D], BF16)
    z2t_h = nc.dram_tensor("z2t_h", [D, T], BF16)
    z2t_l = nc.dram_tensor("z2t_l", [D, T], BF16)
    g_s = nc.dram_tensor("g_s", [D, D], F32)
    ah_s = nc.dram_tensor("ah_s", [D, D], BF16)
    al_s = nc.dram_tensor("al_s", [D, D], BF16)

    SB = nc.alloc_sbuf_tensor_at
    SBUF_BASE = 16896
    off = SBUF_BASE

    def place(name, shape, dt_):
        nonlocal off
        t_ = SB(name, shape, dt_, offset=off)
        sz = shape[1] * {F32: 4, BF16: 2, I32: 4}[dt_]
        off += (sz + 31) & ~31
        return t_

    cst = place("cst", [128, 8], F32)
    stats = place("stats", [128, 128], F32)
    gbc = place("gbc", [128, 8], F32)
    gacc = place("gacc", [128, 72], F32)
    gres = place("gres", [1, 8], F32)
    ident = place("ident", [128, 128], BF16)
    assert off <= SBUF_BASE + 4096, off

    off = SBUF_BASE + 4096
    qt = place("qt", [128, 16 * T], BF16)
    wch_base = off
    wch0 = place("wch0", [128, NKD * HC], BF16)
    wch1 = place("wch1", [128, NKD * HC], BF16)
    w1a = SB("w1a", [128, D], F32, offset=wch_base)
    w1b = SB("w1b", [128, D], F32, offset=wch_base + 8192)
    w1c = SB("w1c", [128, D], F32, offset=wch_base + 16384)
    w1qa = SB("w1qa", [128, D], BF16, offset=wch_base + 24576)
    w1qb = SB("w1qb", [128, D], BF16, offset=wch_base + 28672)

    big0 = place("big0", [128, D], F32)
    big1 = place("big1", [128, D], F32)
    big4i = place("big4i", [128, D], I32)
    big5 = place("big5", [128, D], F32)
    big6 = place("big6", [128, D], F32)
    wpfg = SB("wpfg", [128, NKD * HC], BF16, offset=big0.manual_sbuf_range[0]
              if hasattr(big0, "manual_sbuf_range") else 0)
    wpfv = SB("wpfv", [128, NKD * HC], BF16, offset=big4i.manual_sbuf_range[0]
              if hasattr(big4i, "manual_sbuf_range") else 0)
    bigq = place("bigq", [128, D], BF16)
    bigq2 = place("bigq2", [128, D], BF16)

    ev = {}
    for name, width, dt_ in [
        ("gate0", HC, F32), ("gate1", HC, F32), ("gate2a", HC, F32),
        ("gate2b", HC, F32), ("val0", HC, F32), ("val1", HC, F32),
        ("hh0", HC, F32), ("hh1", HC, F32), ("bgb", HC, F32), ("bvb", HC, F32),
        ("bgb1", HC, F32), ("bvb1", HC, F32),
        ("ost0", HC, F32), ("ost1", HC, F32), ("x2a", HC, F32), ("x2b", HC, F32),
    ]:
        ev[name] = place(name, [128, width], dt_)
    hbo = place("hbo", [128, D], F32)
    assert off <= 207 * 1024, off
    ws0 = SB("ws0", [128, 1024], BF16, offset=ev["gate0"].manual_sbuf_range[0])
    ws1 = SB("ws1", [128, 1024], BF16, offset=ev["gate1"].manual_sbuf_range[0])

    qt3 = qt[:, :].rearrange("p (j x) -> p j x", j=16)

    pb = [nc.alloc_psum_tensor(f"pb{i}", [128, 512], F32) for i in range(6)]
    pt = [nc.alloc_psum_tensor(f"pt{i}", [128, 1024], BF16) for i in range(2)]

    p = Plan()

    with nc.Block() as block:
        sems = {s: nc.alloc_semaphore(name="SEM_" + s) for s in SEM_NAMES}

        c_zero = cst[:, 0:1]
        c_one = cst[:, 1:2]
        c_eps = cst[:, 2:3]
        c_mhalf = cst[:, 4:5]

        def stat(col, tc_):
            return stats[:, col + tc_:col + tc_ + 1]

        p.on("sync", lambda e: e.dma_start(out=cst[:, :], in_=cst_in),
             sem="cst", inc=16)
        p.on("sync", lambda e: e.dma_start(out=ident[:, :], in_=ident_in),
             sem="cst", inc=16)

        p.on("vector", lambda e: e.memzero(gacc[:, :]), sem="cV", inc=1)
        pieces = []
        for (src, rows, mi) in [(gw, HID, 0), (vw, HID, 1), (ow, D, 2)]:
            for r0 in range(0, rows, 128):
                for c0 in range(0, src.shape[1], D):
                    pieces.append((src, r0, c0, mi))
        red_marks = {}
        slot_count = [0, 0, 0]
        for k, (src, r0, c0, mi) in enumerate(pieces):
            buf = [w1a, w1b, w1c][k % 3]
            slm = ["w0l0", "w0l1", "w0l2"][k % 3]
            w = [("cV", red_marks[k - 3])] if k >= 3 else []
            p.on("gpsimd", lambda e, s=src, r=r0, c=c0, b=buf:
                 e.dma_start(out=b[:, :], in_=s[r:r + 128, c:c + D]),
                 sem=slm, inc=16, waits=w)
            mdma = p.mark(slm)
            p.on("vector", lambda e, b=buf:
                 e.reduce_sum(gacc[:, 52:68].rearrange("p (a b) -> p a b",
                                                       b=1),
                              b[:, :].rearrange("p (a b) -> p a b", b=128),
                              AX.X, apply_absolute_value=True),
                 waits=[(slm, mdma)])
            p.on("vector", lambda e:
                 e.reduce_sum(gacc[:, 51:52], gacc[:, 52:68], AX.X))
            slot = slot_count[mi] % 16
            slot_count[mi] += 1
            p.on("vector", lambda e, m=mi, sl2=slot:
                 e.tensor_add(gacc[:, m * 16 + sl2:m * 16 + sl2 + 1],
                              gacc[:, m * 16 + sl2:m * 16 + sl2 + 1],
                              gacc[:, 51:52]), sem="cV", inc=1)
            red_marks[k] = p.mark("cV")
        for mi in range(3):
            base = mi * 16
            for lev, width in [(0, 8), (1, 4), (2, 2), (3, 1)]:
                p.on("vector", lambda e, b=base, w2=width:
                     e.tensor_add(gacc[:, b:b + w2],
                                  gacc[:, b:b + w2],
                                  gacc[:, b + w2:b + 2 * w2]))
            p.on("vector", lambda e, b=base, m=mi:
                 e.tensor_copy(gacc[:, 48 + m:49 + m], gacc[:, b:b + 1]))
        p.on("vector", lambda e: e.nop(), sem="cV", inc=1)
        m_w0v = p.mark("cV")
        p.on("sync", lambda e: e.dma_start(out=gpart_d.ap(),
                                           in_=gacc[:, 48:51]),
             sem="gbcS", inc=16, waits=[("cV", m_w0v)])
        p.on("sync", lambda e:
             e.dma_start(out=w1c[0:3, 0:128],
                         in_=gpart_d.ap().rearrange("a b -> b a"),
                         allow_slow_non_contiguous=True),
             sem="gbcS", inc=16, waits=[("gbcS", 16)])
        p.on("vector", lambda e:
             e.reduce_sum(w1c[0:3, 128:144].rearrange("p (a b) -> p a b", b=1),
                          w1c[0:3, 0:128].rearrange("p (a b) -> p a b", b=8),
                          AX.X),
             waits=[("gbcS", 32), ("cV", m_w0v)])
        p.on("vector", lambda e:
             e.reduce_sum(w1c[0:3, 144:145], w1c[0:3, 128:144], AX.X),
             sem="cV", inc=1)
        m_gmm = p.mark("cV")
        p.on("sync", lambda e: e.dma_start(out=gb3_d.ap(),
                                           in_=w1c[0:3, 144:145]),
             sem="gbcS", inc=16, waits=[("cV", m_gmm)])
        p.on("sync", lambda e:
             e.dma_start(out=gres[0:1, 0:3],
                         in_=gb3_d.ap().rearrange("a b -> b a"),
                         allow_slow_non_contiguous=True),
             sem="gbcS", inc=16, waits=[("gbcS", 48)])
        for mi, numel in [(0, HID * D), (1, HID * D), (2, D * HID)]:
            p.on("vector", lambda e, m=mi, n=numel:
                 e.tensor_scalar(gacc[0:1, 69 + m:70 + m], gres[0:1, m:m + 1],
                                 1.0 / n, 1e-5, ALU.mult, ALU.max),
                 waits=[("gbcS", 64)] if mi == 0 else [])
        for mi in range(3):
            p.on("vector", lambda e, m=mi:
                 e.tensor_scalar_mul(gres[0:1, 2 * m:2 * m + 1],
                                     gacc[0:1, 69 + m:70 + m], 0.5))
            p.on("vector", lambda e, m=mi:
                 e.tensor_scalar_mul(gres[0:1, 2 * m + 1:2 * m + 2],
                                     gacc[0:1, 69 + m:70 + m], -0.5))
        for mi, col in [(0, 6), (1, 7)]:
            p.on("vector", lambda e, m=mi, c=col:
                 e.tensor_single_scalar(gres[0:1, c:c + 1],
                                        gres[0:1, 2 * m:2 * m + 1],
                                        2.0 / 127.0, ALU.mult))
        m_gres = p.mark("cV")
        p.on("sync", lambda e: e.dma_start(out=gb_d.ap(), in_=gres[0:1, :]),
             sem="gbcS", inc=16, waits=[("cV", m_gres)])

        def bcast_gb(e):
            i_ap, o_ap = broadcast_tensor_aps(gb_d.ap(), gbc[:, :])
            return e.dma_start(out=o_ap, in_=i_ap)
        p.on("sync", bcast_gb, sem="gbcS", inc=16, waits=[("gbcS", 80)])

        def bcast_bo(e):
            i_ap, o_ap = broadcast_tensor_aps(bo, hbo[:, :])
            return e.dma_start(out=o_ap, in_=i_ap)
        p.on("sync", bcast_bo, sem="hboS", inc=16)
        p.on("vector", lambda e: e.tensor_scalar_mul(hbo[:, :], hbo[:, :], 0.5),
             sem="cV", inc=1, waits=[("hboS", 16)])
        m_hbo = p.mark("cV")

        w1tiles = []
        for hgrp in range(NHC):
            for (src, dst, pcol, ncol) in [(gw, tg_t, 0, 1), (vw, tv_t, 2, 3)]:
                for rr in range(4):
                    w1tiles.append((src, dst, hgrp * 512 + rr * 128, 0,
                                    pcol, ncol))
        for r0 in range(0, D, 128):
            for c0 in range(0, HID, D):
                w1tiles.append((ow, to_t, r0, c0, 4, 5))
        vm_w1 = {}
        trT = {}
        w1ptf = {0: 0, 1: 0}
        for k, (src, dst, r0, c0, pcol, ncol) in enumerate(w1tiles):
            wbuf = [w1a, w1b][k % 2]
            qbuf = [w1qa, w1qb][k % 2]
            ls = "w1l0" if k % 2 == 0 else "w1l1"
            w = [("cV", vm_w1[k - 2])] if k >= 2 else []
            p.on("sync", lambda e, s=src, r=r0, c=c0, b=wbuf:
                 e.dma_start(out=b[:, :], in_=s[r:r + 128, c:c + D]),
                 sem=ls, inc=16, waits=[("gbcS", 96)] + w)
            mld = p.mark(ls)
            w2 = [("cT", trT[k - 2])] if k >= 2 else [("cV", m_w0v)]
            p.on("vector", lambda e, b=wbuf, pc=pcol:
                 e.tensor_single_scalar(w1c[:, :], b[:, :],
                                        gbc[:, pc:pc + 1], ALU.is_ge),
                 waits=[(ls, mld)] + w2)
            p.on("vector", lambda e, b=wbuf, nc_=ncol:
                 e.tensor_single_scalar(b[:, :], b[:, :],
                                        gbc[:, nc_:nc_ + 1], ALU.is_le))
            p.on("vector", lambda e, b=wbuf, q=qbuf:
                 e.tensor_sub(q[:, :], w1c[:, :], b[:, :]), sem="cV", inc=1)
            vm_w1[k] = p.mark("cV")
            for half in range(2):
                ptb = pt[half]
                stg_w = ws0 if half == 0 else ws1
                ssem = "w1s0" if half == 0 else "w1s1"
                for kk in range(8):
                    jc = half * 8 + kk
                    p.on("tensor", lambda e, j=jc, k2=kk, b=ptb, q=qbuf:
                         e.transpose(b[:, k2 * 128:(k2 + 1) * 128],
                                     q[:, j * 128:(j + 1) * 128],
                                     ident[:, :]),
                         sem="cT" if kk == 7 else None,
                         inc=1 if kk == 7 else 0,
                         waits=[("cV", vm_w1[k]), ("cst", 32),
                                ("cS", w1ptf[half])] if kk == 0 else [])
                mtw = p.mark("cT")
                p.on("scalar", lambda e, b=ptb, s=stg_w:
                     e.copy(s[:, :], b[:, :]), sem="cS", inc=1,
                     waits=[("cT", mtw), (ssem, p.mark(ssem))])
                w1ptf[half] = p.mark("cS")

                def st_w(e, d=dst, r=r0, c=c0, h=half, s=stg_w):
                    dview = d.ap()[c + h * 1024:c + (h + 1) * 1024,
                                   r:r + 128]
                    dr = dview.rearrange("(k p) c -> p k c", p=128)
                    sr = s[:, :].rearrange("p (k c) -> p k c", c=128)
                    return e.dma_start(out=dr, in_=sr)
                p.on("gpsimd", st_w, sem=ssem, inc=16,
                     waits=[("cS", w1ptf[half])])
            trT[k] = p.mark("cT")
        m_w1s = [p.mark("w1s0"), p.mark("w1s1")]

        def ffn(src_ap, dst_ap, residual_ap, src_waits, wch_ready_t,
                first_ffn):
            hst_base = [p.mark("hst0"), p.mark("hst1")]
            acc_base = [p.mark("acc0"), p.mark("acc1")]
            pt_free = {0: p.mark("cS"), 1: p.mark("cS")}
            aV = {0: 0, 1: 0}
            aS = {0: 0, 1: 0}
            for tc_ in range(NTC):
                par_a = tc_ % 2
                abuf = big0 if par_a == 0 else big1
                axsem = "ax" if par_a == 0 else "ax1"
                w = list(src_waits) if tc_ < 2 else []
                if aV[par_a]:
                    w.append(("cV", aV[par_a]))
                if aS[par_a]:
                    w.append(("cS", aS[par_a]))
                p.on("sync", lambda e, t=tc_, b=abuf:
                     e.dma_start(out=b[:, :],
                                 in_=src_ap[t * TC:(t + 1) * TC, :]),
                     sem=axsem, inc=16, waits=w)
                mdx = p.mark(axsem)
                p.on("vector", lambda e, b=abuf:
                     e.reduce_sum(stats[:, 114:115], b[:, :], AX.X),
                     waits=[(axsem, mdx)])
                p.on("vector", lambda e, t=tc_:
                     e.tensor_single_scalar(stat(0, t), stats[:, 114:115],
                                            1.0 / D, ALU.mult))
                p.on("scalar", lambda e, b=abuf:
                     e.activation(big6[:, :], b[:, :], AF.Square,
                                  bias=c_zero, scale=c_one,
                                  accum_out=stats[:, 115:116]),
                     sem="cS", inc=1,
                     waits=[(axsem, mdx), ("cst", 32), ("cV", p.mark("cV"))])
                msq = p.mark("cS")
                p.on("vector", lambda e, b=abuf:
                     e.reduce_max(stats[:, 116:117], b[:, :], AX.X))
                p.on("vector", lambda e, b=abuf:
                     e.tensor_reduce(stats[:, 117:118], b[:, :], AX.X,
                                     ALU.min))
                aV[par_a] = p.mark("cV")
                p.on("vector", lambda e:
                     e.tensor_single_scalar(stats[:, 118:119],
                                            stats[:, 115:116], 1.0 / D,
                                            ALU.mult),
                     waits=[("cS", msq)])
                p.on("vector", lambda e, t=tc_:
                     e.tensor_mul(stats[:, 119:120], stat(0, t), stat(0, t)))
                p.on("vector", lambda e:
                     e.tensor_sub(stats[:, 118:119], stats[:, 118:119],
                                  stats[:, 119:120]), sem="cV", inc=1)
                mvv = p.mark("cV")
                p.on("scalar", lambda e:
                     e.activation(stats[:, 119:120], stats[:, 118:119],
                                  AF.Sqrt, bias=c_eps, scale=c_one),
                     sem="cS", inc=1, waits=[("cV", mvv)])
                msr = p.mark("cS")
                p.on("vector", lambda e, t=tc_:
                     e.reciprocal(stat(16, t), stats[:, 119:120]),
                     waits=[("cS", msr)])
                p.on("vector", lambda e, t=tc_:
                     e.tensor_sub(stats[:, 116:117], stats[:, 116:117],
                                  stat(0, t)))
                p.on("vector", lambda e, t=tc_:
                     e.tensor_sub(stats[:, 117:118], stat(0, t),
                                  stats[:, 117:118]))
                p.on("vector", lambda e:
                     e.tensor_max(stats[:, 116:117], stats[:, 116:117],
                                  stats[:, 117:118]))
                p.on("vector", lambda e, t=tc_:
                     e.tensor_mul(stats[:, 116:117], stat(16, t),
                                  stats[:, 116:117]))
                p.on("vector", lambda e:
                     e.tensor_single_scalar(stats[:, 116:117],
                                            stats[:, 116:117], 1e-5,
                                            ALU.max))
                p.on("vector", lambda e, t=tc_:
                     e.tensor_mul(stat(32, t), stats[:, 116:117], gbc[:, 6:7]),
                     waits=[("gbcS", 96)])
                p.on("vector", lambda e, t=tc_:
                     e.tensor_mul(stat(48, t), stats[:, 116:117], gbc[:, 7:8]))
                p.on("vector", lambda e:
                     e.reciprocal(stats[:, 117:118], stats[:, 116:117]))
                p.on("vector", lambda e:
                     e.tensor_single_scalar(stats[:, 117:118],
                                            stats[:, 117:118], 127.0,
                                            ALU.mult))
                p.on("vector", lambda e, t=tc_:
                     e.tensor_mul(stats[:, 117:118], stats[:, 117:118],
                                  stat(16, t)))
                p.on("vector", lambda e, t=tc_:
                     e.tensor_mul(stats[:, 120:121], stat(0, t),
                                  stats[:, 117:118]))
                p.on("vector", lambda e:
                     e.tensor_single_scalar(stats[:, 120:121],
                                            stats[:, 120:121], 0.5, ALU.add))
                p.on("vector", lambda e:
                     e.tensor_single_scalar(stats[:, 120:121],
                                            stats[:, 120:121], -1.0,
                                            ALU.mult), sem="cV", inc=1)
                mbias = p.mark("cV")
                p.on("scalar", lambda e, b=abuf:
                     e.activation(big5[:, :], b[:, :], AF.Identity,
                                  bias=stats[:, 120:121],
                                  scale=stats[:, 117:118]),
                     sem="cS", inc=1, waits=[("cV", mbias)])
                msc5 = p.mark("cS")
                aS[par_a] = msc5
                p.on("vector", lambda e:
                     e.tensor_copy(big4i[:, :], big5[:, :]),
                     waits=[("cS", msc5)])
                p.on("vector", lambda e:
                     e.tensor_tensor(big6[:, :], big5[:, :], big4i[:, :],
                                     ALU.subtract))
                p.on("vector", lambda e:
                     e.scalar_tensor_tensor(bigq[:, :], big6[:, :], 0.0,
                                            big4i[:, :], ALU.is_ge, ALU.add),
                     sem="cV", inc=1, waits=[("cT", p.mark("cT"))])
                mq = p.mark("cV")
                for half in range(2):
                    ptb = pt[half]
                    for kk in range(8):
                        jc = half * 8 + kk
                        p.on("tensor", lambda e, j=jc, k=kk, b=ptb:
                             e.transpose(b[:, k * 128:(k + 1) * 128],
                                         bigq[:, j * 128:(j + 1) * 128],
                                         ident[:, :]),
                             sem="cT" if kk == 7 else None,
                             inc=1 if kk == 7 else 0,
                             waits=([("cV", mq), ("cst", 32),
                                     ("cS", pt_free[half])]
                                    if kk == 0 else []))
                    mt = p.mark("cT")
                    p.on("scalar", lambda e, h=half, t=tc_, b=ptb:
                         e.copy(
                             qt3[:, h * 8:(h + 1) * 8, t * 128:(t + 1) * 128],
                             b[:, :].rearrange("p (k x) -> p k x", k=8)),
                         sem="cS", inc=1, waits=[("cT", mt)])
                    pt_free[half] = p.mark("cS")
            m_phaseA = p.mark("cV")
            m_phaseA_S = p.mark("cS")

            p.on("vector", lambda e: e.memzero(stats[:, 64:80]), sem="cV",
                 inc=1, waits=[("cV", m_phaseA)])
            psum_free = {}
            tens_done = {}
            wld_marks = {}
            bias_marks = {}
            def emit_b_loads(hc_):
                bufg = wch0 if hc_ % 2 == 0 else wpfg
                bufv = wch1 if hc_ % 2 == 0 else wpfv
                wsem = "wld" if hc_ % 2 == 0 else "wld1"
                bsem = "bias" if hc_ % 2 == 0 else "bias1"
                wv = []
                if hc_ >= 2:
                    wv.append(("cT", tens_done[hc_ - 2]))
                elif first_ffn:
                    wv += [("w1s0", m_w1s[0]), ("w1s1", m_w1s[1])]
                else:
                    wv.append(("cT", wch_ready_t))
                if hc_ == 1:
                    wv.append(("cV", p.mark("cV")))
                    wv.append(("cT", p.mark("cT")))
                    wv.append(("cS", p.mark("cS")))
                for (mat_s, b2) in [(tg_t, bufg), (tv_t, bufv)]:
                    for kd in range(NKD):
                        p.on("sync", lambda e, s=mat_s, h=hc_, k=kd, b=b2:
                             e.dma_start(
                                 out=b[:, k * HC:(k + 1) * HC],
                                 in_=s.ap()[k * 128:(k + 1) * 128,
                                            h * HC:(h + 1) * HC]),
                             sem=wsem, inc=16, waits=wv if kd == 0 else [])
                wld_marks[hc_] = (wsem, p.mark(wsem))

                def bc_bias(e, src, dstbuf, h):
                    i_ap, o_ap = broadcast_tensor_aps(
                        src[0:1, h * HC:(h + 1) * HC], dstbuf[:, :])
                    return e.dma_start(out=o_ap, in_=i_ap)
                bgb2 = ev["bgb"] if hc_ % 2 == 0 else ev["bgb1"]
                bvb2 = ev["bvb"] if hc_ % 2 == 0 else ev["bvb1"]
                wb = [("cV", psum_free[(hc_ - 2) * NTC + NTC - 1])]                     if hc_ >= 2 else []
                p.on("sync", lambda e, h=hc_, d2=bgb2: bc_bias(e, bg, d2, h),
                     sem=bsem, inc=16, waits=wb)
                p.on("sync", lambda e, h=hc_, d2=bvb2: bc_bias(e, bv, d2, h),
                     sem=bsem, inc=16)
                bias_marks[hc_] = (bsem, p.mark(bsem))

            emit_b_loads(0)
            for hc_ in range(NHC):
                if hc_ + 1 < NHC:
                    emit_b_loads(hc_ + 1)
                bufg = wch0 if hc_ % 2 == 0 else wpfg
                bufv = wch1 if hc_ % 2 == 0 else wpfv
                wsem, m_wld = wld_marks[hc_]
                bsem, m_bias = bias_marks[hc_]
                bgb2 = ev["bgb"] if hc_ % 2 == 0 else ev["bgb1"]
                bvb2 = ev["bvb"] if hc_ % 2 == 0 else ev["bvb1"]
                for tc_ in range(NTC):
                    ti = hc_ * NTC + tc_
                    par = ti % 2
                    pg, pv = pb[par], pb[2 + par]
                    wv = [("cV", psum_free[ti - 2])] if ti >= 2 else []
                    for kd in range(NKD):
                        p.on("tensor", lambda e, k=kd, t=tc_, g=pg, b=bufg:
                             e.matmul(g[:, 0:HC],
                                      qt[:, k * T + t * 128:
                                         k * T + (t + 1) * 128],
                                      b[:, k * HC:(k + 1) * HC],
                                      start=(k == 0), stop=(k == NKD - 1)),
                             waits=([(wsem, m_wld), ("cS", m_phaseA_S)] + wv)
                             if kd == 0 else [])
                        p.on("tensor", lambda e, k=kd, t=tc_, v=pv, b=bufv:
                             e.matmul(v[:, 0:HC],
                                      qt[:, k * T + t * 128:
                                         k * T + (t + 1) * 128],
                                      b[:, k * HC:(k + 1) * HC],
                                      start=(k == 0), stop=(k == NKD - 1)),
                             sem="cT" if kd == NKD - 1 else None,
                             inc=1 if kd == NKD - 1 else 0)
                    tens_done[hc_] = p.mark("cT")
                    mt = p.mark("cT")
                    gl = ev["gate0"] if par == 0 else ev["gate1"]
                    vl = ev["val0"] if par == 0 else ev["val1"]
                    g2 = ev["gate2a"] if par == 0 else ev["gate2b"]
                    hb = ev["hh0"] if par == 0 else ev["hh1"]
                    hsem = "hst0" if par == 0 else "hst1"
                    wh = [(hsem, hst_base[par] + 16 * (ti // 2))] \
                        if ti >= 2 else []
                    p.on("vector", lambda e, g=pg, t=tc_, o=gl, bb=bgb2:
                         e.scalar_tensor_tensor(o[:, :], g[:, 0:HC],
                                                stat(32, t), bb[:, :],
                                                ALU.mult, ALU.add),
                         waits=[("cT", mt), (bsem, m_bias)] + wh)
                    p.on("vector", lambda e, v=pv, t=tc_, o=vl, bb=bvb2:
                         e.scalar_tensor_tensor(o[:, :], v[:, 0:HC],
                                                stat(48, t), bb[:, :],
                                                ALU.mult, ALU.add),
                         sem="cV", inc=1)
                    psum_free[ti] = p.mark("cV")
                    mev = p.mark("cV")
                    p.on("scalar", lambda e, i=gl, o=g2:
                         e.activation(o[:, :], i[:, :], AF.Silu,
                                      bias=c_zero, scale=c_one),
                         sem="cS", inc=1, waits=[("cV", mev)])
                    msil = p.mark("cS")
                    p.on("vector", lambda e, a=g2, b=vl, o=hb:
                         e.tensor_mul(o[:, :], a[:, :], b[:, :]),
                         waits=[("cS", msil)])
                    p.on("vector", lambda e, o=hb:
                         e.reduce_max(stats[:, 118:119], o[:, :], AX.X,
                                      apply_absolute_value=True))
                    p.on("vector", lambda e, t=tc_:
                         e.tensor_max(stat(64, t), stat(64, t),
                                      stats[:, 118:119]), sem="cV", inc=1)
                    mh = p.mark("cV")
                    p.on("gpsimd", lambda e, t=tc_, h=hc_, o=hb:
                         e.dma_start(out=h_s.ap()[t * TC:(t + 1) * TC,
                                                  h * HC:(h + 1) * HC],
                                     in_=o[:, :]),
                         sem=hsem, inc=16, waits=[("cV", mh)])
            m_hst = [p.mark("hst0"), p.mark("hst1")]

            p.on("vector", lambda e:
                 e.tensor_single_scalar(stats[:, 64:80], stats[:, 64:80],
                                        1e-5, ALU.max))
            p.on("vector", lambda e:
                 e.tensor_single_scalar(stats[:, 80:96], stats[:, 64:80],
                                        gbc[:, 4:5], ALU.mult))
            p.on("vector", lambda e:
                 e.tensor_single_scalar(stats[:, 80:96], stats[:, 80:96],
                                        1.0 / 127.0, ALU.mult))
            p.on("vector", lambda e:
                 e.reciprocal(stats[:, 96:112], stats[:, 64:80]))
            p.on("vector", lambda e:
                 e.tensor_single_scalar(stats[:, 96:112], stats[:, 96:112],
                                        127.0, ALU.mult), sem="cV", inc=1)
            m_gh = p.mark("cV")

            ostages = [ev["ost0"], ev["ost1"]]
            sc_free = {}
            qev = {}
            to_marks = {}
            tdone_c = {}
            hfree = {0: 0, 1: 0}
            bigq_free = {0: p.mark("cT"), 1: p.mark("cT")}
            big5_free = [0]
            qcnt = [0]

            def emit_c_toload(sl, db):
                gi = sl * NDB + db
                buf = wch0 if gi % 2 == 0 else wch1
                wsem = "wld" if gi % 2 == 0 else "wld1"
                wv = []
                if gi >= 2:
                    wv.append(("cT", tdone_c[gi - 2]))
                else:
                    wv.append(("cT", p.mark("cT")))
                    if first_ffn:
                        wv += [("w1s0", m_w1s[0]), ("w1s1", m_w1s[1])]
                for kd in range(NKD):
                    p.on("scalar", lambda e, s2=sl, d=db, k=kd, b=buf:
                         e.dma_start(
                             out=b[:, k * HC:(k + 1) * HC],
                             in_=to_t.ap()[s2 * SLAB + k * 128:
                                           s2 * SLAB + (k + 1) * 128,
                                           d * 512:(d + 1) * 512]),
                         sem=wsem, inc=16, waits=wv if kd == 0 else [])
                to_marks[gi] = (wsem, p.mark(wsem))

            chain_out = {}

            def emit_c_chain(sl, tc_):
                par_h = tc_ % 2
                par_q = qcnt[0] % 2
                qcnt[0] += 1
                qb = bigq if par_q == 0 else bigq2
                hbuf = big0 if par_h == 0 else big1
                hsem = "hld" if par_h == 0 else "hld1"
                w = [("hst0", m_hst[0]), ("hst1", m_hst[1]), ("cV", m_gh)]
                if hfree[par_h]:
                    w.append(("cS", hfree[par_h]))
                p.on("sync", lambda e, s=sl, t=tc_, b=hbuf:
                     e.dma_start(out=b[:, :],
                                 in_=h_s.ap()[t * TC:(t + 1) * TC,
                                              s * SLAB:(s + 1) * SLAB]),
                     sem=hsem, inc=16, waits=w)
                mdh = p.mark(hsem)
                p.on("scalar", lambda e, t=tc_, b=hbuf:
                     e.activation(big5[:, :], b[:, :], AF.Identity,
                                  bias=c_mhalf, scale=stat(96, t)),
                     sem="cS", inc=1,
                     waits=[(hsem, mdh), ("cV", max(m_gh, big5_free[0]))])
                msc = p.mark("cS")
                hfree[par_h] = msc
                p.on("vector", lambda e:
                     e.tensor_copy(big4i[:, :], big5[:, :]),
                     waits=[("cS", msc)])
                p.on("vector", lambda e:
                     e.tensor_tensor(big6[:, :], big5[:, :], big4i[:, :],
                                     ALU.subtract), sem="cV", inc=1)
                big5_free[0] = p.mark("cV")
                p.on("vector", lambda e, q=qb:
                     e.scalar_tensor_tensor(q[:, :], big6[:, :], 0.0,
                                            big4i[:, :], ALU.is_ge,
                                            ALU.add),
                     sem="cV", inc=1, waits=[("cT", bigq_free[par_q])])
                chain_out[(sl, tc_)] = (qb, par_q, p.mark("cV"))

            def emit_c_tpose(sl, tc_):
                qb, par_q, mq = chain_out[(sl, tc_)]
                for half in range(2):
                    ptb = pt[half]
                    for kk in range(8):
                        jc = half * 8 + kk
                        p.on("tensor", lambda e, j=jc, k=kk, b=ptb, q=qb:
                             e.transpose(b[:, k * 128:(k + 1) * 128],
                                         q[:, j * 128:(j + 1) * 128],
                                         ident[:, :]),
                             sem="cT" if kk == 7 else None,
                             inc=1 if kk == 7 else 0,
                             waits=[("cV", mq),
                                    ("cS", pt_free[half])]
                             if kk == 0 else [])
                    mt = p.mark("cT")
                    p.on("scalar", lambda e, h=half, t=tc_, b=ptb:
                         e.copy(
                             qt3[:, h * 8:(h + 1) * 8,
                                 t * 128:(t + 1) * 128],
                             b[:, :].rearrange("p (k x) -> p k x", k=8)),
                         sem="cS", inc=1, waits=[("cT", mt)])
                    pt_free[half] = p.mark("cS")
                bigq_free[par_q] = p.mark("cT")
                qev[(sl, tc_)] = p.mark("cS")

            nacc = 0
            emit_c_chain(0, 0)
            emit_c_chain(0, 1)
            for t0 in range(NTC):
                emit_c_tpose(0, t0)
                if t0 + 2 < NTC:
                    emit_c_chain(0, t0 + 2)
            for sl in range(NSLAB):
                macc_prev = (p.mark("acc0"), p.mark("acc1"))
                accop = ALU.bypass if sl == 0 else ALU.add
                if sl == 0:
                    emit_c_toload(0, 0)
                for db in range(NDB):
                    gi = sl * NDB + db
                    if gi + 1 < NSLAB * NDB:
                        emit_c_toload((gi + 1) // NDB, (gi + 1) % NDB)
                    if db == NDB - 1 and sl + 1 < NSLAB:
                        emit_c_chain(sl + 1, 0)
                        emit_c_chain(sl + 1, 1)
                    wsem_c, m_told = to_marks[gi]
                    bufc = wch0 if gi % 2 == 0 else wch1
                    for tc_ in range(NTC):
                        ti2 = db * NTC + tc_
                        par = ti2 % 2
                        po = pb[4] if par == 0 else pb[5]
                        if ti2 >= 2:
                            wv2 = [("cS", sc_free[(sl, ti2 - 2)])]
                        elif sl > 0:
                            wv2 = [("cS", sc_free[(sl - 1,
                                                   NTC * NDB - 2 + ti2)])]
                        else:
                            wv2 = []
                        for kd in range(NKD):
                            p.on("tensor", lambda e, k=kd, t=tc_, o=po,
                                 b=bufc:
                                 e.matmul(o[:, 0:512],
                                          qt[:, k * T + t * 128:
                                             k * T + (t + 1) * 128],
                                          b[:, k * HC:(k + 1) * HC],
                                          start=(k == 0), stop=(k == NKD - 1)),
                                 sem="cT" if kd == NKD - 1 else None,
                                 inc=1 if kd == NKD - 1 else 0,
                                 waits=([(wsem_c, m_told),
                                         ("cS", qev[(sl, tc_)])] + wv2)
                                 if kd == 0 else [])
                        mt = p.mark("cT")
                        tdone_c[gi] = mt
                        ost = ostages[par]
                        asem = "acc0" if par == 0 else "acc1"
                        wc = [(asem, acc_base[par] + 16 * (nacc // 2))] \
                            if nacc >= 2 else []
                        nacc += 1
                        p.on("scalar", lambda e, o=po, st=ost:
                             e.copy(st[:, :], o[:, 0:512]),
                             sem="cS", inc=1, waits=[("cT", mt)] + wc)
                        sc_free[(sl, ti2)] = p.mark("cS")
                        mcp = p.mark("cS")
                        wacc = [("cS", mcp)]
                        if sl > 0 and db == 0 and tc_ == 0:
                            wacc += [("acc0", macc_prev[0]),
                                     ("acc1", macc_prev[1])]
                        p.on("gpsimd", lambda e, t=tc_, d=db, st=ost, a=accop:
                             e.dma_start(out=acc_s.ap()[t * TC:(t + 1) * TC,
                                                        d * 512:(d + 1) * 512],
                                         in_=st[:, :], accum_op=a),
                             sem=asem, inc=16, waits=wacc)
                        if db == NDB - 1 and sl + 1 < NSLAB:
                            emit_c_tpose(sl + 1, tc_)
                            if tc_ + 2 < NTC:
                                emit_c_chain(sl + 1, tc_ + 2)
            m_acc = [p.mark("acc0"), p.mark("acc1")]

            for tc_ in range(NTC):
                w = [("acc0", m_acc[0]), ("acc1", m_acc[1])]
                if tc_ >= 1:
                    w.append(("cV", p.mark("cV")))
                    w.append(("dst", p.mark("dst")))
                p.on("sync", lambda e, t=tc_:
                     e.dma_start(out=big0[:, :],
                                 in_=acc_s.ap()[t * TC:(t + 1) * TC, :]),
                     sem="dacc", inc=16, waits=w)
                p.on("sync", lambda e, t=tc_:
                     e.dma_start(out=big1[:, :],
                                 in_=residual_ap[t * TC:(t + 1) * TC, :]),
                     sem="dacc", inc=16)
                mdd = p.mark("dacc")
                p.on("vector", lambda e, t=tc_:
                     e.scalar_tensor_tensor(big6[:, :], big0[:, :],
                                            stat(80, t), hbo[:, :],
                                            ALU.mult, ALU.add),
                     waits=[("dacc", mdd), ("cV", m_hbo)])
                p.on("vector", lambda e:
                     e.tensor_add(big0[:, :], big6[:, :], big1[:, :]),
                     sem="cV", inc=1)
                mvd = p.mark("cV")
                p.on("gpsimd", lambda e, t=tc_:
                     e.dma_start(out=dst_ap[t * TC:(t + 1) * TC, :],
                                 in_=big0[:, :]),
                     sem="dst", inc=16, waits=[("cV", mvd)])
            return p.mark("dst")

        if stages == "ffn1":
            m_dst = ffn(x_in, out_d, x_in, [], 0, True)
            p.on("gpsimd", lambda e: e.wait_ge(sems["dst"], m_dst))
            def emit(eng_name, engine):
                for waits, fn, sem, inc in p.steps[eng_name]:
                    for (s2, v) in waits:
                        engine.wait_ge(sems[s2], v)
                    inst = fn(engine)
                    if inc and inst is not None:
                        inst.then_inc(sems[sem], inc)
            @block.sync
            def _(sync):
                emit("sync", sync)
            @block.scalar
            def _(scalar):
                emit("scalar", scalar)
            @block.vector
            def _(vector):
                emit("vector", vector)
            @block.tensor
            def _(tensor):
                emit("tensor", tensor)
            @block.gpsimd
            def _(gpsimd):
                emit("gpsimd", gpsimd)
            return nc
        m_ffn1_dst = ffn(x_in, x2_s.ap(), x_in, [], 0, True)

        for tc_ in range(NTC):
            w = [("dst", m_ffn1_dst)]
            if tc_ >= 1:
                w += [("cV", p.mark("cV")), ("cS", p.mark("cS"))]
            p.on("sync", lambda e, t=tc_:
                 e.dma_start(out=big0[:, :],
                             in_=x2_s.ap()[t * TC:(t + 1) * TC, :]),
                 sem="ex", inc=16, waits=w)
            mdx = p.mark("ex")
            p.on("vector", lambda e:
                 e.reduce_sum(stats[:, 114:115], big0[:, :], AX.X),
                 waits=[("ex", mdx)])
            p.on("vector", lambda e:
                 e.tensor_single_scalar(stats[:, 114:115], stats[:, 114:115],
                                        1.0 / D, ALU.mult))
            p.on("scalar", lambda e:
                 e.activation(big1[:, :], big0[:, :], AF.Square,
                              bias=c_zero, scale=c_one,
                              accum_out=stats[:, 115:116]),
                 sem="cS", inc=1, waits=[("ex", mdx), ("cV", p.mark("cV"))])
            msq = p.mark("cS")
            p.on("vector", lambda e:
                 e.tensor_single_scalar(stats[:, 118:119], stats[:, 115:116],
                                        1.0 / D, ALU.mult),
                 waits=[("cS", msq)])
            p.on("vector", lambda e:
                 e.tensor_mul(stats[:, 119:120], stats[:, 114:115],
                              stats[:, 114:115]))
            p.on("vector", lambda e:
                 e.tensor_sub(stats[:, 118:119], stats[:, 118:119],
                              stats[:, 119:120]), sem="cV", inc=1)
            mvv = p.mark("cV")
            p.on("scalar", lambda e:
                 e.activation(stats[:, 119:120], stats[:, 118:119], AF.Sqrt,
                              bias=c_eps, scale=c_one),
                 sem="cS", inc=1, waits=[("cV", mvv)])
            msr = p.mark("cS")
            p.on("vector", lambda e:
                 e.reciprocal(stats[:, 118:119], stats[:, 119:120]),
                 waits=[("cS", msr)])
            p.on("vector", lambda e:
                 e.tensor_scalar(big5[:, :], big0[:, :], stats[:, 114:115],
                                 stats[:, 118:119], ALU.subtract, ALU.mult),
                 waits=[("cS", p.mark("cS"))])
            p.on("scalar", lambda e: e.copy(bigq[:, :], big5[:, :]),
                 sem="cS", inc=1,
                 waits=[("cT", p.mark("cT")), ("est", p.mark("est")),
                        ("cV", p.mark("cV"))])
            mzh = p.mark("cS")
            p.on("vector", lambda e:
                 e.tensor_tensor(bigq2[:, :], big5[:, :], bigq[:, :],
                                 ALU.subtract), sem="cV", inc=1,
                 waits=[("cS", mzh)])
            mz = p.mark("cV")
            p.on("gpsimd", lambda e, t=tc_:
                 e.dma_start(out=z2hl.ap()[t * TC:(t + 1) * TC, 0:D],
                             in_=bigq[:, :]), sem="est", inc=16,
                 waits=[("cS", mzh)])
            p.on("gpsimd", lambda e, t=tc_:
                 e.dma_start(out=z2hl.ap()[t * TC:(t + 1) * TC, D:2 * D],
                             in_=bigq2[:, :]), sem="est", inc=16,
                 waits=[("cV", mz)])
            for (qbuf, dstt) in [(bigq, z2t_h), (bigq2, z2t_l)]:
                for half in range(2):
                    ptb = pt[half]
                    for kk in range(8):
                        jc = half * 8 + kk
                        p.on("tensor", lambda e, j=jc, k=kk, b=ptb, q=qbuf:
                             e.transpose(b[:, k * 128:(k + 1) * 128],
                                         q[:, j * 128:(j + 1) * 128],
                                         ident[:, :]),
                             sem="cT" if kk == 7 else None,
                             inc=1 if kk == 7 else 0,
                             waits=[("cV", max(mz, p.mark("cV"))),
                                    ("cS", p.mark("cS"))]
                             if kk == 0 else [])
                    mtt = p.mark("cT")
                    stg = wch0 if half == 0 else wch1
                    if half == 0:
                        p.on("scalar", lambda e, b=ptb, s=stg:
                             e.copy(s[:, 0:1024], b[:, :]),
                             sem="cS", inc=1,
                             waits=[("cT", mtt), ("est", p.mark("est"))])
                        mvt = ("cS", p.mark("cS"))
                    else:
                        p.on("vector", lambda e, b=ptb, s=stg:
                             e.tensor_copy(s[:, 0:1024], b[:, :]),
                             sem="cV", inc=1,
                             waits=[("cT", mtt), ("est", p.mark("est"))])
                        mvt = ("cV", p.mark("cV"))

                    def st_t(e, t=tc_, h=half, s=stg, d=dstt):
                        dview = d.ap()[h * 1024:(h + 1) * 1024,
                                       t * 128:(t + 1) * 128]
                        dr = dview.rearrange("(k p) c -> p k c", p=128)
                        sr = s[:, 0:1024].rearrange("p (k c) -> p k c", c=128)
                        return e.dma_start(out=dr, in_=sr)
                    p.on("gpsimd", st_t, sem="est", inc=16,
                         waits=[mvt])
        m_est = p.mark("est")

        def do_ag(e):
            e.collective_compute(
                "AllGather", ALU.bypass,
                replica_groups=[[0, 1], [2, 3], [4, 5], [6, 7]],
                ins=[z2hl.ap().opt()],
                outs=[z2full.ap().opt()]).then_inc(sems["cc"], 1)
            return None
        p.on("gpsimd", do_ag, waits=[("est", m_est)])

        for jbi in range(NJB):
            wv = [("cT", p.mark("cT"))]
            for (hl, base) in [(0, 0), (1, D)]:
                for cc2 in range(NCC):
                    p.on("sync", lambda e, j=jbi, c=cc2, b=base, o=hl:
                         e.dma_start(
                             out=qt[:, (o * NCC + c) * 512:
                                    (o * NCC + c + 1) * 512],
                             in_=z2full.ap()[c * 128:(c + 1) * 128,
                                             b + j * 512:b + (j + 1) * 512]),
                         sem="jb", inc=16,
                         waits=(wv + [("cc", 1)])
                         if (cc2 == 0 and hl == 0) else [])
            m_jb = p.mark("jb")
            for ic in range(16):
                ti = jbi * 16 + ic
                ibh = wch0 if ic % 2 == 0 else wch1
                zsem = "zi0" if ic % 2 == 0 else "zi1"
                wv2 = [("cT", p.mark("cT"))]
                for (hl, base) in [(0, 0), (1, D)]:
                    def ld_i(e, i=ic, b=base, buf=ibh, o=hl):
                        src = z2full.ap()[0:CT, b + i * 128:b + (i + 1) * 128]
                        sr = src.rearrange("(c p) i -> p c i", p=128)
                        dst2 = buf[:, o * NCC * 128:(o + 1) * NCC * 128]
                        dr = dst2.rearrange("p (c i) -> p c i", i=128)
                        return e.dma_start(out=dr, in_=sr)
                    p.on("sync", ld_i, sem=zsem, inc=16,
                         waits=([("jb", m_jb)] + wv2) if hl == 0 else [])
                m_ic = p.mark(zsem)
                par = ti % 2
                pg = pb[par]
                gsem = "gst0" if par == 0 else "gst1"
                wv3 = [("cV", p.mark("cV"))] if ti >= 2 else []
                passes = [(c3, la, ra) for c3 in range(NCC)
                          for (la, ra) in [(0, 0), (0, 1), (1, 0)]]
                for idx, (c3, la, ra) in enumerate(passes):
                    last = idx == len(passes) - 1
                    p.on("tensor", lambda e, lo=(la * NCC + c3) * 128,
                         ro=(ra * NCC + c3) * 512, g=pg, b=ibh,
                         st=(idx == 0), sp=last:
                         e.matmul(g[:, 0:512], b[:, lo:lo + 128],
                                  qt[:, ro:ro + 512], start=st, stop=sp),
                         sem="cT" if last else None, inc=1 if last else 0,
                         waits=([(zsem, m_ic), ("jb", m_jb)] + wv3)
                         if idx == 0 else [])
                mt = p.mark("cT")
                stg = ev["ost0"] if par == 0 else ev["ost1"]
                wg = [(gsem, 16 * (ti // 2))] if ti >= 2 else []
                p.on("vector", lambda e, g=pg, s=stg:
                     e.tensor_copy(s[:, :], g[:, 0:512]), sem="cV", inc=1,
                     waits=[("cT", mt)] + wg)
                mve = p.mark("cV")
                p.on("gpsimd", lambda e, i=ic, j=jbi, s=stg:
                     e.dma_start(out=g_s.ap()[i * 128:(i + 1) * 128,
                                              j * 512:(j + 1) * 512],
                                 in_=s[:, :]),
                     sem=gsem, inc=16, waits=[("cV", mve)])
        m_gst = [p.mark("gst0"), p.mark("gst1")]
        m_gram_t = p.mark("cT")

        for ic in range(16):
            w = [("gst0", m_gst[0]), ("gst1", m_gst[1])]
            if ic >= 1:
                w += [("cV", p.mark("cV")), ("cS", p.mark("cS"))]
            p.on("sync", lambda e, i=ic:
                 e.dma_start(out=big0[:, :],
                             in_=g_s.ap()[i * 128:(i + 1) * 128, :]),
                 sem="gld", inc=16, waits=w)
            mdg = p.mark("gld")
            p.on("vector", lambda e:
                 e.reduce_max(stats[:, 114:115], big0[:, :], AX.X),
                 waits=[("gld", mdg)])
            p.on("vector", lambda e:
                 e.tensor_single_scalar(stats[:, 115:116], stats[:, 114:115],
                                        -1.0, ALU.mult), sem="cV", inc=1)
            mvm = p.mark("cV")
            p.on("scalar", lambda e:
                 e.activation(big1[:, :], big0[:, :], AF.Exp,
                              bias=stats[:, 115:116], scale=c_one,
                              accum_out=stats[:, 116:117]),
                 sem="cS", inc=1, waits=[("cV", mvm)])
            mse = p.mark("cS")
            p.on("vector", lambda e:
                 e.reciprocal(stats[:, 117:118], stats[:, 116:117]),
                 waits=[("cS", mse)])
            p.on("vector", lambda e:
                 e.tensor_single_scalar(big6[:, :], big1[:, :],
                                        stats[:, 117:118], ALU.mult),
                 waits=[("cS", p.mark("cS"))])
            p.on("scalar", lambda e: e.copy(bigq[:, :], big6[:, :]),
                 sem="cS", inc=1,
                 waits=[("ast", p.mark("ast")), ("cV", p.mark("cV"))])
            mha = p.mark("cS")
            p.on("vector", lambda e:
                 e.tensor_tensor(bigq2[:, :], big6[:, :], bigq[:, :],
                                 ALU.subtract), sem="cV", inc=1,
                 waits=[("cS", mha)])
            mva = p.mark("cV")
            p.on("gpsimd", lambda e, i=ic:
                 e.dma_start(out=ah_s.ap()[i * 128:(i + 1) * 128, :],
                             in_=bigq[:, :]), sem="ast", inc=16,
                 waits=[("cS", mha)])
            p.on("gpsimd", lambda e, i=ic:
                 e.dma_start(out=al_s.ap()[i * 128:(i + 1) * 128, :],
                             in_=bigq2[:, :]), sem="ast", inc=16,
                 waits=[("cV", mva)])
        m_ast = p.mark("ast")

        ctx_add_mark = {}
        for ib in range(NIB):
            wv = [("cT", p.mark("cT"))]
            for (hl, src) in [(0, ah_s), (1, al_s)]:
                for jc in range(16):
                    p.on("scalar", lambda e, i=ib, j=jc, s=src, o=hl:
                         e.dma_start(
                             out=qt[:, (o * 16 + j) * 512:
                                    (o * 16 + j + 1) * 512],
                             in_=s.ap()[i * 512:(i + 1) * 512,
                                        j * 128:(j + 1) * 128],
                             transpose=True),
                         sem="wld", inc=16,
                         waits=([("ast", m_ast)] + wv)
                         if (jc == 0 and hl == 0) else [])
            m_at = p.mark("wld")
            for cc2 in range(NTC):
                ti = ib * NTC + cc2
                parb = cc2 % 2
                zb = wch0 if parb == 0 else wch1
                ztsem = "zt0" if parb == 0 else "zt1"
                xsem = "x2l0" if parb == 0 else "x2l1"
                wv2 = [("cT", p.mark("cT"))] if ti >= 2 else \
                    [("cT", m_gram_t)]
                for (hl, src) in [(0, z2t_h), (1, z2t_l)]:
                    def ld_zt(e, c=cc2, s=src, o=hl, buf=zb):
                        sv = s.ap()[0:D, c * 128:(c + 1) * 128]
                        sr = sv.rearrange("(j p) c -> p j c", p=128)
                        dst2 = buf[:, o * 16 * 128:(o + 1) * 16 * 128]
                        dr = dst2.rearrange("p (j c) -> p j c", c=128)
                        return e.dma_start(out=dr, in_=sr)
                    p.on("sync", ld_zt, sem=ztsem, inc=16,
                         waits=([("est", m_est)] + wv2) if hl == 0 else [])
                m_zt = p.mark(ztsem)
                x2t = ev["x2a"] if parb == 0 else ev["x2b"]
                wx = [("cV", ctx_add_mark[ti - 2])] if ti >= 2 else []
                p.on("sync", lambda e, c=cc2, i=ib, o=x2t:
                     e.dma_start(out=o[:, :],
                                 in_=x2_s.ap()[c * 128:(c + 1) * 128,
                                               i * 512:(i + 1) * 512]),
                     sem=xsem, inc=16, waits=wx)
                m_x2 = p.mark(xsem)
                par = ti % 2
                po = pb[2 + par]
                xssem = "xst0" if par == 0 else "xst1"
                wv3 = [("cV", p.mark("cV"))] if ti >= 2 else []
                passes = [(jc, la, ra) for jc in range(16)
                          for (la, ra) in [(0, 0), (0, 1), (1, 0)]]
                for idx, (jc, la, ra) in enumerate(passes):
                    last = idx == len(passes) - 1
                    p.on("tensor", lambda e, lo=(la * 16 + jc) * 128,
                         ro=(ra * 16 + jc) * 512, o=po, b=zb,
                         st=(idx == 0), sp=last:
                         e.matmul(o[:, 0:512], b[:, lo:lo + 128],
                                  qt[:, ro:ro + 512], start=st, stop=sp),
                         sem="cT" if last else None, inc=1 if last else 0,
                         waits=([(ztsem, m_zt), ("wld", m_at)] + wv3)
                         if idx == 0 else [])
                mt = p.mark("cT")
                stg = ev["ost0"] if par == 0 else ev["ost1"]
                wst = [(xssem, 16 * (ti // 2))] if ti >= 2 else []
                p.on("vector", lambda e, o=po, s=stg, xx=x2t:
                     e.tensor_add(s[:, :], o[:, 0:512], xx[:, :]),
                     sem="cV", inc=1,
                     waits=[("cT", mt), (xsem, m_x2)] + wst)
                ctx_add_mark[ti] = p.mark("cV")
                mve = p.mark("cV")
                p.on("gpsimd", lambda e, c=cc2, i=ib, s=stg:
                     e.dma_start(out=x3_s.ap()[c * 128:(c + 1) * 128,
                                               i * 512:(i + 1) * 512],
                                 in_=s[:, :]),
                     sem=xssem, inc=16, waits=[("cV", mve)])
        m_xst = [p.mark("xst0"), p.mark("xst1")]
        m_ctx_t = p.mark("cT")

        m_ffn3_dst = ffn(x3_s.ap(), out_d, x3_s.ap(),
                         [("xst0", m_xst[0]), ("xst1", m_xst[1])],
                         m_ctx_t, False)

        p.on("gpsimd", lambda e: e.wait_ge(sems["dst"], m_ffn3_dst))

        def emit(eng_name, engine):
            for waits, fn, sem, inc in p.steps[eng_name]:
                for (s2, v) in waits:
                    engine.wait_ge(sems[s2], v)
                inst = fn(engine)
                if inc and inst is not None:
                    inst.then_inc(sems[sem], inc)

        @block.sync
        def _(sync):
            emit("sync", sync)

        @block.scalar
        def _(scalar):
            emit("scalar", scalar)

        @block.vector
        def _(vector):
            emit("vector", vector)

        @block.tensor
        def _(tensor):
            emit("tensor", tensor)

        @block.gpsimd
        def _(gpsimd):
            emit("gpsimd", gpsimd)

    return nc


_NC_CACHE = None


def kernel(L0_attn_x, L0_ffn_gate_w, L0_ffn_gate_b, L0_ffn_val_w,
           L0_ffn_val_b, L0_ffn_out_w, L0_ffn_out_b):
    global _NC_CACHE
    x = np.ascontiguousarray(np.asarray(L0_attn_x, dtype=np.float32))
    gw_a = np.ascontiguousarray(np.asarray(L0_ffn_gate_w, dtype=np.float32))
    vw_a = np.ascontiguousarray(np.asarray(L0_ffn_val_w, dtype=np.float32))
    ow_a = np.ascontiguousarray(np.asarray(L0_ffn_out_w, dtype=np.float32))
    bg_a = np.ascontiguousarray(
        np.asarray(L0_ffn_gate_b, dtype=np.float32).reshape(1, HID))
    bv_a = np.ascontiguousarray(
        np.asarray(L0_ffn_val_b, dtype=np.float32).reshape(1, HID))
    bo_a = np.ascontiguousarray(
        np.asarray(L0_ffn_out_b, dtype=np.float32).reshape(1, D))
    ident = np.eye(128, dtype=ml_dtypes.bfloat16)
    cstrow = np.array([0.0, 1.0, 1e-5, 0.5, -0.5, 1.0 / D, 127.0, 0.0],
                      np.float32)
    cst_a = np.ascontiguousarray(np.broadcast_to(cstrow, (128, 8)))

    if _NC_CACHE is None:
        _NC_CACHE = build_program()
    nc = _NC_CACHE

    in_maps = []
    for c in range(8):
        b, hh = divmod(c, 2)
        in_maps.append({
            "x": np.ascontiguousarray(x[b, hh * T:(hh + 1) * T, :]),
            "gw": gw_a, "vw": vw_a, "ow": ow_a,
            "bg": bg_a, "bv": bv_a, "bo": bo_a,
            "ident": ident, "cst": cst_a,
        })
    res = run_bass_kernel_spmd(nc, in_maps, core_ids=list(range(8)))
    out = np.empty((4, 4096, D), np.float32)
    for c in range(8):
        b, hh = divmod(c, 2)
        out[b, hh * T:(hh + 1) * T, :] = np.asarray(res.results[c]["out"])
    return out


# version 11
# speedup vs baseline: 1.1237x; 1.0271x over previous
import numpy as np
import ml_dtypes
import concourse.bass as bass
import concourse.mybir as mybir
from concourse.bass import broadcast_tensor_aps
from concourse.bass_utils import run_bass_kernel_spmd

F32 = mybir.dt.float32
BF16 = mybir.dt.bfloat16
I32 = mybir.dt.int32
AF = mybir.ActivationFunctionType
ALU = mybir.AluOpType
AX = mybir.AxisListType

D = 2048
HID = 8192
T = 2048
TC = 128
NTC = 16
NKD = 16
HC = 512
NHC = 16
SLAB = 2048
NSLAB = 4
NDB = 4
NJB = 4
NIB = 4
CT = 4096
NCC = 32

ENGINES = ("sync", "scalar", "vector", "tensor", "gpsimd")

SEM_NAMES = [
    "cS", "cV", "cT", "cG", "cc",
    "cst", "gbcS", "hboS",
    "w0l0", "w0l1", "w0l2", "w1l0", "w1l1", "w1s0", "w1s1",
    "ax", "ax1", "wld", "wld1", "bias", "bias1", "hst0", "hst1", "hld",
    "hld1", "acc0", "acc1",
    "dacc", "dacc1", "dst", "dst1", "ex", "ex1", "est", "jb",
    "zi0", "zi1", "gst0", "gst1",
    "gld", "gld1", "ast", "zt0", "zt1", "x2l0", "x2l1", "xst0", "xst1",
]


class Plan:
    def __init__(self):
        self.steps = {e: [] for e in ENGINES}
        self.val = {s: 0 for s in SEM_NAMES}

    def on(self, eng, fn, sem=None, inc=0, waits=()):
        waits = list(waits)
        if eng == "vector" and sem in (None, "cV"):
            waits.append(("cV", self.val["cV"]))
            sem, inc = "cV", 1
        elif eng == "scalar" and sem in (None, "cS"):
            waits.append(("cS", self.val["cS"]))
            sem, inc = "cS", 1
        ws = []
        seen = set()
        for (s, v) in waits:
            if s != "cc":
                assert v <= self.val[s], (eng, s, v, self.val[s])
            if v > 0 and (s, v) not in seen:
                seen.add((s, v))
                ws.append((s, v))
        self.steps[eng].append((tuple(ws), fn, sem, inc))
        if inc:
            self.val[sem] += inc
        return self.val[sem] if sem else 0

    def mark(self, sem):
        return self.val[sem]


def build_program(stages='all'):
    nc = bass.Bass("TRN2", target_bir_lowering=False)

    x_in = nc.dram_tensor("x", [T, D], F32, kind="ExternalInput").ap()
    gw = nc.dram_tensor("gw", [HID, D], F32, kind="ExternalInput").ap()
    vw = nc.dram_tensor("vw", [HID, D], F32, kind="ExternalInput").ap()
    ow = nc.dram_tensor("ow", [D, HID], F32, kind="ExternalInput").ap()
    bg = nc.dram_tensor("bg", [1, HID], F32, kind="ExternalInput").ap()
    bv = nc.dram_tensor("bv", [1, HID], F32, kind="ExternalInput").ap()
    bo = nc.dram_tensor("bo", [1, D], F32, kind="ExternalInput").ap()
    ident_in = nc.dram_tensor("ident", [128, 128], BF16, kind="ExternalInput").ap()
    cst_in = nc.dram_tensor("cst", [128, 8], F32, kind="ExternalInput").ap()
    out_d = nc.dram_tensor("out", [T, D], F32, kind="ExternalOutput").ap()

    tg_t = nc.dram_tensor("tg_t", [D, HID], BF16)
    tv_t = nc.dram_tensor("tv_t", [D, HID], BF16)
    to_t = nc.dram_tensor("to_t", [HID, D], BF16)
    h_s = nc.dram_tensor("h_s", [T, HID], F32)
    x2_s = nc.dram_tensor("x2_s", [T, D], F32)
    x3_s = nc.dram_tensor("x3_s", [T, D], F32)
    acc_s = nc.dram_tensor("acc_s", [T, D], F32)
    gb_d = nc.dram_tensor("gb_d", [1, 8], F32)
    gb3_d = nc.dram_tensor("gb3_d", [3, 1], F32)
    gpart_d = nc.dram_tensor("gpart_d", [128, 3], F32)
    z2hl = nc.dram_tensor("z2hl", [T, 2 * D], BF16)
    z2full = nc.dram_tensor("z2full", [CT, 2 * D], BF16)
    z2t_h = nc.dram_tensor("z2t_h", [D, T], BF16)
    z2t_l = nc.dram_tensor("z2t_l", [D, T], BF16)
    g_s = nc.dram_tensor("g_s", [D, D], F32)
    ah_s = nc.dram_tensor("ah_s", [D, D], BF16)
    al_s = nc.dram_tensor("al_s", [D, D], BF16)

    SB = nc.alloc_sbuf_tensor_at
    SBUF_BASE = 16896
    off = SBUF_BASE

    def place(name, shape, dt_):
        nonlocal off
        t_ = SB(name, shape, dt_, offset=off)
        sz = shape[1] * {F32: 4, BF16: 2, I32: 4}[dt_]
        off += (sz + 31) & ~31
        return t_

    cst = place("cst", [128, 8], F32)
    stats = place("stats", [128, 128], F32)
    gbc = place("gbc", [128, 8], F32)
    gacc = place("gacc", [128, 72], F32)
    gres = place("gres", [1, 8], F32)
    ident = place("ident", [128, 128], BF16)
    assert off <= SBUF_BASE + 4096, off

    off = SBUF_BASE + 4096
    qt = place("qt", [128, 16 * T], BF16)
    wch_base = off
    wch0 = place("wch0", [128, NKD * HC], BF16)
    wch1 = place("wch1", [128, NKD * HC], BF16)
    w1a = SB("w1a", [128, D], F32, offset=wch_base)
    w1b = SB("w1b", [128, D], F32, offset=wch_base + 8192)
    w1c = SB("w1c", [128, D], F32, offset=wch_base + 16384)
    w1qa = SB("w1qa", [128, D], BF16, offset=wch_base + 24576)
    w1qb = SB("w1qb", [128, D], BF16, offset=wch_base + 28672)

    big0 = place("big0", [128, D], F32)
    big1 = place("big1", [128, D], F32)
    big4i = place("big4i", [128, D], I32)
    big5 = place("big5", [128, D], F32)
    big6 = place("big6", [128, D], F32)
    wpfg = SB("wpfg", [128, NKD * HC], BF16, offset=big0.manual_sbuf_range[0]
              if hasattr(big0, "manual_sbuf_range") else 0)
    wpfv = SB("wpfv", [128, NKD * HC], BF16, offset=big4i.manual_sbuf_range[0]
              if hasattr(big4i, "manual_sbuf_range") else 0)
    bigq = place("bigq", [128, D], BF16)
    bigq2 = place("bigq2", [128, D], BF16)
    big4f = SB("big4f", [128, D], F32, offset=big4i.manual_sbuf_range[0])

    ev = {}
    for name, width, dt_ in [
        ("gate0", HC, F32), ("gate1", HC, F32), ("gate2a", HC, F32),
        ("gate2b", HC, F32), ("val0", HC, F32), ("val1", HC, F32),
        ("hh0", HC, F32), ("hh1", HC, F32), ("bgb", HC, F32), ("bvb", HC, F32),
        ("bgb1", HC, F32), ("bvb1", HC, F32),
        ("ost0", HC, F32), ("ost1", HC, F32), ("x2a", HC, F32), ("x2b", HC, F32),
    ]:
        ev[name] = place(name, [128, width], dt_)
    hbo = place("hbo", [128, D], F32)
    assert off <= 207 * 1024, off
    ws0 = SB("ws0", [128, 1024], BF16, offset=ev["gate0"].manual_sbuf_range[0])
    ws1 = SB("ws1", [128, 1024], BF16, offset=ev["gate1"].manual_sbuf_range[0])

    qt3 = qt[:, :].rearrange("p (j x) -> p j x", j=16)

    pb = [nc.alloc_psum_tensor(f"pb{i}", [128, 512], F32) for i in range(6)]
    pt = [nc.alloc_psum_tensor(f"pt{i}", [128, 1024], BF16) for i in range(2)]

    p = Plan()

    with nc.Block() as block:
        sems = {s: nc.alloc_semaphore(name="SEM_" + s) for s in SEM_NAMES}

        c_zero = cst[:, 0:1]
        c_one = cst[:, 1:2]
        c_eps = cst[:, 2:3]
        c_mhalf = cst[:, 4:5]

        def stat(col, tc_):
            return stats[:, col + tc_:col + tc_ + 1]

        p.on("sync", lambda e: e.dma_start(out=cst[:, :], in_=cst_in),
             sem="cst", inc=16)
        p.on("sync", lambda e: e.dma_start(out=ident[:, :], in_=ident_in),
             sem="cst", inc=16)

        p.on("vector", lambda e: e.memzero(gacc[:, :]), sem="cV", inc=1)
        pieces = []
        for (src, rows, mi) in [(gw, HID, 0), (vw, HID, 1), (ow, D, 2)]:
            for r0 in range(0, rows, 128):
                for c0 in range(0, src.shape[1], D):
                    pieces.append((src, r0, c0, mi))
        red_marks = {}
        slot_count = [0, 0, 0]
        for k, (src, r0, c0, mi) in enumerate(pieces):
            buf = [w1a, w1b, w1c][k % 3]
            slm = ["w0l0", "w0l1", "w0l2"][k % 3]
            w = [("cV", red_marks[k - 3])] if k >= 3 else []
            p.on("gpsimd", lambda e, s=src, r=r0, c=c0, b=buf:
                 e.dma_start(out=b[:, :], in_=s[r:r + 128, c:c + D]),
                 sem=slm, inc=16, waits=w)
            mdma = p.mark(slm)
            p.on("vector", lambda e, b=buf:
                 e.reduce_sum(gacc[:, 52:68].rearrange("p (a b) -> p a b",
                                                       b=1),
                              b[:, :].rearrange("p (a b) -> p a b", b=128),
                              AX.X, apply_absolute_value=True),
                 waits=[(slm, mdma)])
            p.on("vector", lambda e:
                 e.reduce_sum(gacc[:, 51:52], gacc[:, 52:68], AX.X))
            slot = slot_count[mi] % 16
            slot_count[mi] += 1
            p.on("vector", lambda e, m=mi, sl2=slot:
                 e.tensor_add(gacc[:, m * 16 + sl2:m * 16 + sl2 + 1],
                              gacc[:, m * 16 + sl2:m * 16 + sl2 + 1],
                              gacc[:, 51:52]), sem="cV", inc=1)
            red_marks[k] = p.mark("cV")
        for mi in range(3):
            base = mi * 16
            for lev, width in [(0, 8), (1, 4), (2, 2), (3, 1)]:
                p.on("vector", lambda e, b=base, w2=width:
                     e.tensor_add(gacc[:, b:b + w2],
                                  gacc[:, b:b + w2],
                                  gacc[:, b + w2:b + 2 * w2]))
            p.on("vector", lambda e, b=base, m=mi:
                 e.tensor_copy(gacc[:, 48 + m:49 + m], gacc[:, b:b + 1]))
        p.on("vector", lambda e: e.nop(), sem="cV", inc=1)
        m_w0v = p.mark("cV")
        p.on("sync", lambda e: e.dma_start(out=gpart_d.ap(),
                                           in_=gacc[:, 48:51]),
             sem="gbcS", inc=16, waits=[("cV", m_w0v)])
        p.on("sync", lambda e:
             e.dma_start(out=w1c[0:3, 0:128],
                         in_=gpart_d.ap().rearrange("a b -> b a"),
                         allow_slow_non_contiguous=True),
             sem="gbcS", inc=16, waits=[("gbcS", 16)])
        p.on("vector", lambda e:
             e.reduce_sum(w1c[0:3, 128:144].rearrange("p (a b) -> p a b", b=1),
                          w1c[0:3, 0:128].rearrange("p (a b) -> p a b", b=8),
                          AX.X),
             waits=[("gbcS", 32), ("cV", m_w0v)])
        p.on("vector", lambda e:
             e.reduce_sum(w1c[0:3, 144:145], w1c[0:3, 128:144], AX.X),
             sem="cV", inc=1)
        m_gmm = p.mark("cV")
        p.on("sync", lambda e: e.dma_start(out=gb3_d.ap(),
                                           in_=w1c[0:3, 144:145]),
             sem="gbcS", inc=16, waits=[("cV", m_gmm)])
        p.on("sync", lambda e:
             e.dma_start(out=gres[0:1, 0:3],
                         in_=gb3_d.ap().rearrange("a b -> b a"),
                         allow_slow_non_contiguous=True),
             sem="gbcS", inc=16, waits=[("gbcS", 48)])
        for mi, numel in [(0, HID * D), (1, HID * D), (2, D * HID)]:
            p.on("vector", lambda e, m=mi, n=numel:
                 e.tensor_scalar(gacc[0:1, 69 + m:70 + m], gres[0:1, m:m + 1],
                                 1.0 / n, 1e-5, ALU.mult, ALU.max),
                 waits=[("gbcS", 64)] if mi == 0 else [])
        for mi in range(3):
            p.on("vector", lambda e, m=mi:
                 e.tensor_scalar_mul(gres[0:1, 2 * m:2 * m + 1],
                                     gacc[0:1, 69 + m:70 + m], 0.5))
            p.on("vector", lambda e, m=mi:
                 e.tensor_scalar_mul(gres[0:1, 2 * m + 1:2 * m + 2],
                                     gacc[0:1, 69 + m:70 + m], -0.5))
        for mi, col in [(0, 6), (1, 7)]:
            p.on("vector", lambda e, m=mi, c=col:
                 e.tensor_single_scalar(gres[0:1, c:c + 1],
                                        gres[0:1, 2 * m:2 * m + 1],
                                        2.0 / 127.0, ALU.mult))
        m_gres = p.mark("cV")
        p.on("sync", lambda e: e.dma_start(out=gb_d.ap(), in_=gres[0:1, :]),
             sem="gbcS", inc=16, waits=[("cV", m_gres)])

        def bcast_gb(e):
            i_ap, o_ap = broadcast_tensor_aps(gb_d.ap(), gbc[:, :])
            return e.dma_start(out=o_ap, in_=i_ap)
        p.on("sync", bcast_gb, sem="gbcS", inc=16, waits=[("gbcS", 80)])

        def bcast_bo(e):
            i_ap, o_ap = broadcast_tensor_aps(bo, hbo[:, :])
            return e.dma_start(out=o_ap, in_=i_ap)
        p.on("sync", bcast_bo, sem="hboS", inc=16)
        p.on("vector", lambda e: e.tensor_scalar_mul(hbo[:, :], hbo[:, :], 0.5),
             sem="cV", inc=1, waits=[("hboS", 16)])
        m_hbo = p.mark("cV")

        w1tiles = []
        for hgrp in range(NHC):
            for (src, dst, pcol, ncol) in [(gw, tg_t, 0, 1), (vw, tv_t, 2, 3)]:
                for rr in range(4):
                    w1tiles.append((src, dst, hgrp * 512 + rr * 128, 0,
                                    pcol, ncol))
        for r0 in range(0, D, 128):
            for c0 in range(0, HID, D):
                w1tiles.append((ow, to_t, r0, c0, 4, 5))
        vm_w1 = {}
        trT = {}
        w1ptf = {0: 0, 1: 0}
        for k, (src, dst, r0, c0, pcol, ncol) in enumerate(w1tiles):
            wbuf = [w1a, w1b][k % 2]
            qbuf = [w1qa, w1qb][k % 2]
            ls = "w1l0" if k % 2 == 0 else "w1l1"
            w = [("cV", vm_w1[k - 2])] if k >= 2 else []
            p.on("sync", lambda e, s=src, r=r0, c=c0, b=wbuf:
                 e.dma_start(out=b[:, :], in_=s[r:r + 128, c:c + D]),
                 sem=ls, inc=16, waits=[("gbcS", 96)] + w)
            mld = p.mark(ls)
            w2 = [("cT", trT[k - 2])] if k >= 2 else [("cV", m_w0v)]
            p.on("vector", lambda e, b=wbuf, pc=pcol:
                 e.tensor_single_scalar(w1c[:, :], b[:, :],
                                        gbc[:, pc:pc + 1], ALU.is_ge),
                 waits=[(ls, mld)] + w2)
            p.on("vector", lambda e, b=wbuf, nc_=ncol:
                 e.tensor_single_scalar(b[:, :], b[:, :],
                                        gbc[:, nc_:nc_ + 1], ALU.is_le))
            p.on("vector", lambda e, b=wbuf, q=qbuf:
                 e.tensor_sub(q[:, :], w1c[:, :], b[:, :]), sem="cV", inc=1)
            vm_w1[k] = p.mark("cV")
            for half in range(2):
                ptb = pt[half]
                stg_w = ws0 if half == 0 else ws1
                ssem = "w1s0" if half == 0 else "w1s1"
                for kk in range(8):
                    jc = half * 8 + kk
                    p.on("tensor", lambda e, j=jc, k2=kk, b=ptb, q=qbuf:
                         e.transpose(b[:, k2 * 128:(k2 + 1) * 128],
                                     q[:, j * 128:(j + 1) * 128],
                                     ident[:, :]),
                         sem="cT" if kk == 7 else None,
                         inc=1 if kk == 7 else 0,
                         waits=[("cV", vm_w1[k]), ("cst", 32),
                                ("cS", w1ptf[half])] if kk == 0 else [])
                mtw = p.mark("cT")
                p.on("scalar", lambda e, b=ptb, s=stg_w:
                     e.copy(s[:, :], b[:, :]), sem="cS", inc=1,
                     waits=[("cT", mtw), (ssem, p.mark(ssem))])
                w1ptf[half] = p.mark("cS")

                def st_w(e, d=dst, r=r0, c=c0, h=half, s=stg_w):
                    dview = d.ap()[c + h * 1024:c + (h + 1) * 1024,
                                   r:r + 128]
                    dr = dview.rearrange("(k p) c -> p k c", p=128)
                    sr = s[:, :].rearrange("p (k c) -> p k c", c=128)
                    return e.dma_start(out=dr, in_=sr)
                p.on("gpsimd", st_w, sem=ssem, inc=16,
                     waits=[("cS", w1ptf[half])])
            trT[k] = p.mark("cT")
        m_w1s = [p.mark("w1s0"), p.mark("w1s1")]

        def ffn(src_ap, dst_ap, residual_ap, src_waits, wch_ready_t,
                first_ffn):
            hst_base = [p.mark("hst0"), p.mark("hst1")]
            acc_base = [p.mark("acc0"), p.mark("acc1")]
            pt_free = {0: p.mark("cS"), 1: p.mark("cS")}
            aV = {0: 0, 1: 0}
            aS = {0: 0, 1: 0}
            for tc_ in range(NTC):
                par_a = tc_ % 2
                abuf = big0 if par_a == 0 else big1
                axsem = "ax" if par_a == 0 else "ax1"
                w = list(src_waits) if tc_ < 2 else []
                if aV[par_a]:
                    w.append(("cV", aV[par_a]))
                if aS[par_a]:
                    w.append(("cS", aS[par_a]))
                p.on("sync", lambda e, t=tc_, b=abuf:
                     e.dma_start(out=b[:, :],
                                 in_=src_ap[t * TC:(t + 1) * TC, :]),
                     sem=axsem, inc=16, waits=w)
                mdx = p.mark(axsem)
                p.on("vector", lambda e, b=abuf:
                     e.reduce_sum(stats[:, 114:115], b[:, :], AX.X),
                     waits=[(axsem, mdx)])
                p.on("vector", lambda e, t=tc_:
                     e.tensor_single_scalar(stat(0, t), stats[:, 114:115],
                                            1.0 / D, ALU.mult))
                p.on("scalar", lambda e, b=abuf:
                     e.activation(big6[:, :], b[:, :], AF.Square,
                                  bias=c_zero, scale=c_one,
                                  accum_out=stats[:, 115:116]),
                     sem="cS", inc=1,
                     waits=[(axsem, mdx), ("cst", 32), ("cV", p.mark("cV"))])
                msq = p.mark("cS")
                p.on("vector", lambda e, b=abuf:
                     e.reduce_max(stats[:, 116:117], b[:, :], AX.X))
                p.on("vector", lambda e, b=abuf:
                     e.tensor_reduce(stats[:, 117:118], b[:, :], AX.X,
                                     ALU.min))
                aV[par_a] = p.mark("cV")
                p.on("vector", lambda e:
                     e.tensor_single_scalar(stats[:, 118:119],
                                            stats[:, 115:116], 1.0 / D,
                                            ALU.mult),
                     waits=[("cS", msq)])
                p.on("vector", lambda e, t=tc_:
                     e.tensor_mul(stats[:, 119:120], stat(0, t), stat(0, t)))
                p.on("vector", lambda e:
                     e.tensor_sub(stats[:, 118:119], stats[:, 118:119],
                                  stats[:, 119:120]), sem="cV", inc=1)
                mvv = p.mark("cV")
                p.on("scalar", lambda e:
                     e.activation(stats[:, 119:120], stats[:, 118:119],
                                  AF.Sqrt, bias=c_eps, scale=c_one),
                     sem="cS", inc=1, waits=[("cV", mvv)])
                msr = p.mark("cS")
                p.on("vector", lambda e, t=tc_:
                     e.reciprocal(stat(16, t), stats[:, 119:120]),
                     waits=[("cS", msr)])
                p.on("vector", lambda e, t=tc_:
                     e.tensor_sub(stats[:, 116:117], stats[:, 116:117],
                                  stat(0, t)))
                p.on("vector", lambda e, t=tc_:
                     e.tensor_sub(stats[:, 117:118], stat(0, t),
                                  stats[:, 117:118]))
                p.on("vector", lambda e:
                     e.tensor_max(stats[:, 116:117], stats[:, 116:117],
                                  stats[:, 117:118]))
                p.on("vector", lambda e, t=tc_:
                     e.tensor_mul(stats[:, 116:117], stat(16, t),
                                  stats[:, 116:117]))
                p.on("vector", lambda e:
                     e.tensor_single_scalar(stats[:, 116:117],
                                            stats[:, 116:117], 1e-5,
                                            ALU.max))
                p.on("vector", lambda e, t=tc_:
                     e.tensor_mul(stat(32, t), stats[:, 116:117], gbc[:, 6:7]),
                     waits=[("gbcS", 96)])
                p.on("vector", lambda e, t=tc_:
                     e.tensor_mul(stat(48, t), stats[:, 116:117], gbc[:, 7:8]))
                p.on("vector", lambda e:
                     e.reciprocal(stats[:, 117:118], stats[:, 116:117]))
                p.on("vector", lambda e:
                     e.tensor_single_scalar(stats[:, 117:118],
                                            stats[:, 117:118], 127.0,
                                            ALU.mult))
                p.on("vector", lambda e, t=tc_:
                     e.tensor_mul(stats[:, 117:118], stats[:, 117:118],
                                  stat(16, t)))
                p.on("vector", lambda e, t=tc_:
                     e.tensor_mul(stats[:, 120:121], stat(0, t),
                                  stats[:, 117:118]))
                p.on("vector", lambda e:
                     e.tensor_single_scalar(stats[:, 120:121],
                                            stats[:, 120:121], 0.5, ALU.add))
                p.on("vector", lambda e:
                     e.tensor_single_scalar(stats[:, 120:121],
                                            stats[:, 120:121], -1.0,
                                            ALU.mult), sem="cV", inc=1)
                mbias = p.mark("cV")
                p.on("scalar", lambda e, b=abuf:
                     e.activation(big5[:, :], b[:, :], AF.Identity,
                                  bias=stats[:, 120:121],
                                  scale=stats[:, 117:118]),
                     sem="cS", inc=1, waits=[("cV", mbias)])
                msc5 = p.mark("cS")
                aS[par_a] = msc5
                p.on("vector", lambda e:
                     e.tensor_copy(big4i[:, :], big5[:, :]),
                     waits=[("cS", msc5)])
                p.on("vector", lambda e:
                     e.tensor_tensor(big6[:, :], big5[:, :], big4i[:, :],
                                     ALU.subtract))
                p.on("vector", lambda e:
                     e.scalar_tensor_tensor(bigq[:, :], big6[:, :], 0.0,
                                            big4i[:, :], ALU.is_ge, ALU.add),
                     sem="cV", inc=1, waits=[("cT", p.mark("cT"))])
                mq = p.mark("cV")
                for half in range(2):
                    ptb = pt[half]
                    for kk in range(8):
                        jc = half * 8 + kk
                        p.on("tensor", lambda e, j=jc, k=kk, b=ptb:
                             e.transpose(b[:, k * 128:(k + 1) * 128],
                                         bigq[:, j * 128:(j + 1) * 128],
                                         ident[:, :]),
                             sem="cT" if kk == 7 else None,
                             inc=1 if kk == 7 else 0,
                             waits=([("cV", mq), ("cst", 32),
                                     ("cS", pt_free[half])]
                                    if kk == 0 else []))
                    mt = p.mark("cT")
                    p.on("scalar", lambda e, h=half, t=tc_, b=ptb:
                         e.copy(
                             qt3[:, h * 8:(h + 1) * 8, t * 128:(t + 1) * 128],
                             b[:, :].rearrange("p (k x) -> p k x", k=8)),
                         sem="cS", inc=1, waits=[("cT", mt)])
                    pt_free[half] = p.mark("cS")
            m_phaseA = p.mark("cV")
            m_phaseA_S = p.mark("cS")

            p.on("vector", lambda e: e.memzero(stats[:, 64:80]), sem="cV",
                 inc=1, waits=[("cV", m_phaseA)])
            psum_free = {}
            tens_done = {}
            wld_marks = {}
            bias_marks = {}
            def emit_b_loads(hc_):
                bufg = wch0 if hc_ % 2 == 0 else wpfg
                bufv = wch1 if hc_ % 2 == 0 else wpfv
                wsem = "wld" if hc_ % 2 == 0 else "wld1"
                bsem = "bias" if hc_ % 2 == 0 else "bias1"
                wv = []
                if hc_ >= 2:
                    wv.append(("cT", tens_done[hc_ - 2]))
                elif first_ffn:
                    wv += [("w1s0", m_w1s[0]), ("w1s1", m_w1s[1])]
                else:
                    wv.append(("cT", wch_ready_t))
                if hc_ == 1:
                    wv.append(("cV", p.mark("cV")))
                    wv.append(("cT", p.mark("cT")))
                    wv.append(("cS", p.mark("cS")))
                for (mat_s, b2) in [(tg_t, bufg), (tv_t, bufv)]:
                    for kd in range(NKD):
                        p.on("sync", lambda e, s=mat_s, h=hc_, k=kd, b=b2:
                             e.dma_start(
                                 out=b[:, k * HC:(k + 1) * HC],
                                 in_=s.ap()[k * 128:(k + 1) * 128,
                                            h * HC:(h + 1) * HC]),
                             sem=wsem, inc=16, waits=wv if kd == 0 else [])
                wld_marks[hc_] = (wsem, p.mark(wsem))

                def bc_bias(e, src, dstbuf, h):
                    i_ap, o_ap = broadcast_tensor_aps(
                        src[0:1, h * HC:(h + 1) * HC], dstbuf[:, :])
                    return e.dma_start(out=o_ap, in_=i_ap)
                bgb2 = ev["bgb"] if hc_ % 2 == 0 else ev["bgb1"]
                bvb2 = ev["bvb"] if hc_ % 2 == 0 else ev["bvb1"]
                wb = [("cV", psum_free[(hc_ - 2) * NTC + NTC - 1])]                     if hc_ >= 2 else []
                p.on("sync", lambda e, h=hc_, d2=bgb2: bc_bias(e, bg, d2, h),
                     sem=bsem, inc=16, waits=wb)
                p.on("sync", lambda e, h=hc_, d2=bvb2: bc_bias(e, bv, d2, h),
                     sem=bsem, inc=16)
                bias_marks[hc_] = (bsem, p.mark(bsem))

            emit_b_loads(0)
            for hc_ in range(NHC):
                if hc_ + 1 < NHC:
                    emit_b_loads(hc_ + 1)
                bufg = wch0 if hc_ % 2 == 0 else wpfg
                bufv = wch1 if hc_ % 2 == 0 else wpfv
                wsem, m_wld = wld_marks[hc_]
                bsem, m_bias = bias_marks[hc_]
                bgb2 = ev["bgb"] if hc_ % 2 == 0 else ev["bgb1"]
                bvb2 = ev["bvb"] if hc_ % 2 == 0 else ev["bvb1"]
                for tc_ in range(NTC):
                    ti = hc_ * NTC + tc_
                    par = ti % 2
                    pg, pv = pb[par], pb[2 + par]
                    wv = [("cV", psum_free[ti - 2])] if ti >= 2 else []
                    for kd in range(NKD):
                        p.on("tensor", lambda e, k=kd, t=tc_, g=pg, b=bufg:
                             e.matmul(g[:, 0:HC],
                                      qt[:, k * T + t * 128:
                                         k * T + (t + 1) * 128],
                                      b[:, k * HC:(k + 1) * HC],
                                      start=(k == 0), stop=(k == NKD - 1)),
                             waits=([(wsem, m_wld), ("cS", m_phaseA_S)] + wv)
                             if kd == 0 else [])
                        p.on("tensor", lambda e, k=kd, t=tc_, v=pv, b=bufv:
                             e.matmul(v[:, 0:HC],
                                      qt[:, k * T + t * 128:
                                         k * T + (t + 1) * 128],
                                      b[:, k * HC:(k + 1) * HC],
                                      start=(k == 0), stop=(k == NKD - 1)),
                             sem="cT" if kd == NKD - 1 else None,
                             inc=1 if kd == NKD - 1 else 0)
                    tens_done[hc_] = p.mark("cT")
                    mt = p.mark("cT")
                    gl = ev["gate0"] if par == 0 else ev["gate1"]
                    vl = ev["val0"] if par == 0 else ev["val1"]
                    g2 = ev["gate2a"] if par == 0 else ev["gate2b"]
                    hb = ev["hh0"] if par == 0 else ev["hh1"]
                    hsem = "hst0" if par == 0 else "hst1"
                    wh = [(hsem, hst_base[par] + 16 * (ti // 2))] \
                        if ti >= 2 else []
                    p.on("vector", lambda e, g=pg, t=tc_, o=gl, bb=bgb2:
                         e.scalar_tensor_tensor(o[:, :], g[:, 0:HC],
                                                stat(32, t), bb[:, :],
                                                ALU.mult, ALU.add),
                         waits=[("cT", mt), (bsem, m_bias)] + wh)
                    p.on("vector", lambda e, v=pv, t=tc_, o=vl, bb=bvb2:
                         e.scalar_tensor_tensor(o[:, :], v[:, 0:HC],
                                                stat(48, t), bb[:, :],
                                                ALU.mult, ALU.add),
                         sem="cV", inc=1)
                    psum_free[ti] = p.mark("cV")
                    mev = p.mark("cV")
                    p.on("scalar", lambda e, i=gl, o=g2:
                         e.activation(o[:, :], i[:, :], AF.Silu,
                                      bias=c_zero, scale=c_one),
                         sem="cS", inc=1, waits=[("cV", mev)])
                    msil = p.mark("cS")
                    p.on("vector", lambda e, a=g2, b=vl, o=hb:
                         e.tensor_mul(o[:, :], a[:, :], b[:, :]),
                         waits=[("cS", msil)])
                    p.on("vector", lambda e, o=hb:
                         e.reduce_max(stats[:, 118:119], o[:, :], AX.X,
                                      apply_absolute_value=True))
                    p.on("vector", lambda e, t=tc_:
                         e.tensor_max(stat(64, t), stat(64, t),
                                      stats[:, 118:119]), sem="cV", inc=1)
                    mh = p.mark("cV")
                    p.on("gpsimd", lambda e, t=tc_, h=hc_, o=hb:
                         e.dma_start(out=h_s.ap()[t * TC:(t + 1) * TC,
                                                  h * HC:(h + 1) * HC],
                                     in_=o[:, :]),
                         sem=hsem, inc=16, waits=[("cV", mh)])
            m_hst = [p.mark("hst0"), p.mark("hst1")]

            p.on("vector", lambda e:
                 e.tensor_single_scalar(stats[:, 64:80], stats[:, 64:80],
                                        1e-5, ALU.max))
            p.on("vector", lambda e:
                 e.tensor_single_scalar(stats[:, 80:96], stats[:, 64:80],
                                        gbc[:, 4:5], ALU.mult))
            p.on("vector", lambda e:
                 e.tensor_single_scalar(stats[:, 80:96], stats[:, 80:96],
                                        1.0 / 127.0, ALU.mult))
            p.on("vector", lambda e:
                 e.reciprocal(stats[:, 96:112], stats[:, 64:80]))
            p.on("vector", lambda e:
                 e.tensor_single_scalar(stats[:, 96:112], stats[:, 96:112],
                                        127.0, ALU.mult), sem="cV", inc=1)
            m_gh = p.mark("cV")

            ostages = [ev["ost0"], ev["ost1"]]
            sc_free = {}
            qev = {}
            to_marks = {}
            tdone_c = {}
            hfree = {0: 0, 1: 0}
            bigq_free = {0: p.mark("cT"), 1: p.mark("cT")}
            big5_free = [0]
            qcnt = [0]

            def emit_c_toload(sl, db):
                gi = sl * NDB + db
                buf = wch0 if gi % 2 == 0 else wch1
                wsem = "wld" if gi % 2 == 0 else "wld1"
                wv = []
                if gi >= 2:
                    wv.append(("cT", tdone_c[gi - 2]))
                else:
                    wv.append(("cT", p.mark("cT")))
                    if first_ffn:
                        wv += [("w1s0", m_w1s[0]), ("w1s1", m_w1s[1])]
                for kd in range(NKD):
                    p.on("scalar", lambda e, s2=sl, d=db, k=kd, b=buf:
                         e.dma_start(
                             out=b[:, k * HC:(k + 1) * HC],
                             in_=to_t.ap()[s2 * SLAB + k * 128:
                                           s2 * SLAB + (k + 1) * 128,
                                           d * 512:(d + 1) * 512]),
                         sem=wsem, inc=16, waits=wv if kd == 0 else [])
                to_marks[gi] = (wsem, p.mark(wsem))

            chain_out = {}

            def emit_c_chain(sl, tc_):
                par_h = tc_ % 2
                par_q = qcnt[0] % 2
                qcnt[0] += 1
                qb = bigq if par_q == 0 else bigq2
                hbuf = big0 if par_h == 0 else big1
                hsem = "hld" if par_h == 0 else "hld1"
                w = [("hst0", m_hst[0]), ("hst1", m_hst[1]), ("cV", m_gh)]
                if hfree[par_h]:
                    w.append(("cS", hfree[par_h]))
                p.on("sync", lambda e, s=sl, t=tc_, b=hbuf:
                     e.dma_start(out=b[:, :],
                                 in_=h_s.ap()[t * TC:(t + 1) * TC,
                                              s * SLAB:(s + 1) * SLAB]),
                     sem=hsem, inc=16, waits=w)
                mdh = p.mark(hsem)
                p.on("scalar", lambda e, t=tc_, b=hbuf:
                     e.activation(big5[:, :], b[:, :], AF.Identity,
                                  bias=c_mhalf, scale=stat(96, t)),
                     sem="cS", inc=1,
                     waits=[(hsem, mdh), ("cV", max(m_gh, big5_free[0]))])
                msc = p.mark("cS")
                hfree[par_h] = msc
                p.on("vector", lambda e:
                     e.tensor_copy(big4i[:, :], big5[:, :]),
                     waits=[("cS", msc)])
                p.on("vector", lambda e:
                     e.tensor_tensor(big6[:, :], big5[:, :], big4i[:, :],
                                     ALU.subtract), sem="cV", inc=1)
                big5_free[0] = p.mark("cV")
                p.on("vector", lambda e, q=qb:
                     e.scalar_tensor_tensor(q[:, :], big6[:, :], 0.0,
                                            big4i[:, :], ALU.is_ge,
                                            ALU.add),
                     sem="cV", inc=1, waits=[("cT", bigq_free[par_q])])
                chain_out[(sl, tc_)] = (qb, par_q, p.mark("cV"))

            def emit_c_tpose(sl, tc_):
                qb, par_q, mq = chain_out[(sl, tc_)]
                for half in range(2):
                    ptb = pt[half]
                    for kk in range(8):
                        jc = half * 8 + kk
                        p.on("tensor", lambda e, j=jc, k=kk, b=ptb, q=qb:
                             e.transpose(b[:, k * 128:(k + 1) * 128],
                                         q[:, j * 128:(j + 1) * 128],
                                         ident[:, :]),
                             sem="cT" if kk == 7 else None,
                             inc=1 if kk == 7 else 0,
                             waits=[("cV", mq),
                                    ("cS", pt_free[half])]
                             if kk == 0 else [])
                    mt = p.mark("cT")
                    p.on("scalar", lambda e, h=half, t=tc_, b=ptb:
                         e.copy(
                             qt3[:, h * 8:(h + 1) * 8,
                                 t * 128:(t + 1) * 128],
                             b[:, :].rearrange("p (k x) -> p k x", k=8)),
                         sem="cS", inc=1, waits=[("cT", mt)])
                    pt_free[half] = p.mark("cS")
                bigq_free[par_q] = p.mark("cT")
                qev[(sl, tc_)] = p.mark("cS")

            nacc = 0
            emit_c_chain(0, 0)
            emit_c_chain(0, 1)
            for t0 in range(NTC):
                emit_c_tpose(0, t0)
                if t0 + 2 < NTC:
                    emit_c_chain(0, t0 + 2)
            for sl in range(NSLAB):
                macc_prev = (p.mark("acc0"), p.mark("acc1"))
                accop = ALU.bypass if sl == 0 else ALU.add
                if sl == 0:
                    emit_c_toload(0, 0)
                for db in range(NDB):
                    gi = sl * NDB + db
                    if gi + 1 < NSLAB * NDB:
                        emit_c_toload((gi + 1) // NDB, (gi + 1) % NDB)
                    if db == NDB - 1 and sl + 1 < NSLAB:
                        emit_c_chain(sl + 1, 0)
                        emit_c_chain(sl + 1, 1)
                    wsem_c, m_told = to_marks[gi]
                    bufc = wch0 if gi % 2 == 0 else wch1
                    for tc_ in range(NTC):
                        ti2 = db * NTC + tc_
                        par = ti2 % 2
                        po = pb[4] if par == 0 else pb[5]
                        if ti2 >= 2:
                            wv2 = [("cS", sc_free[(sl, ti2 - 2)])]
                        elif sl > 0:
                            wv2 = [("cS", sc_free[(sl - 1,
                                                   NTC * NDB - 2 + ti2)])]
                        else:
                            wv2 = []
                        for kd in range(NKD):
                            p.on("tensor", lambda e, k=kd, t=tc_, o=po,
                                 b=bufc:
                                 e.matmul(o[:, 0:512],
                                          qt[:, k * T + t * 128:
                                             k * T + (t + 1) * 128],
                                          b[:, k * HC:(k + 1) * HC],
                                          start=(k == 0), stop=(k == NKD - 1)),
                                 sem="cT" if kd == NKD - 1 else None,
                                 inc=1 if kd == NKD - 1 else 0,
                                 waits=([(wsem_c, m_told),
                                         ("cS", qev[(sl, tc_)])] + wv2)
                                 if kd == 0 else [])
                        mt = p.mark("cT")
                        tdone_c[gi] = mt
                        ost = ostages[par]
                        asem = "acc0" if par == 0 else "acc1"
                        wc = [(asem, acc_base[par] + 16 * (nacc // 2))] \
                            if nacc >= 2 else []
                        nacc += 1
                        p.on("scalar", lambda e, o=po, st=ost:
                             e.copy(st[:, :], o[:, 0:512]),
                             sem="cS", inc=1, waits=[("cT", mt)] + wc)
                        sc_free[(sl, ti2)] = p.mark("cS")
                        mcp = p.mark("cS")
                        wacc = [("cS", mcp)]
                        if sl > 0 and db == 0 and tc_ == 0:
                            wacc += [("acc0", macc_prev[0]),
                                     ("acc1", macc_prev[1])]
                        p.on("gpsimd", lambda e, t=tc_, d=db, st=ost, a=accop:
                             e.dma_start(out=acc_s.ap()[t * TC:(t + 1) * TC,
                                                        d * 512:(d + 1) * 512],
                                         in_=st[:, :], accum_op=a),
                             sem=asem, inc=16, waits=wacc)
                        if db == NDB - 1 and sl + 1 < NSLAB:
                            emit_c_tpose(sl + 1, tc_)
                            if tc_ + 2 < NTC:
                                emit_c_chain(sl + 1, tc_ + 2)
            m_acc = [p.mark("acc0"), p.mark("acc1")]

            dV = {0: 0, 1: 0}
            dfree = {0: 0, 1: 0}
            for tc_ in range(NTC):
                par_d = tc_ % 2
                abuf2 = big0 if par_d == 0 else big4f
                rbuf = big1 if par_d == 0 else big5
                dsem = "dacc" if par_d == 0 else "dacc1"
                ssem = "dst" if par_d == 0 else "dst1"
                w = [("acc0", m_acc[0]), ("acc1", m_acc[1])]
                if dV[par_d]:
                    w.append(("cV", dV[par_d]))
                    w.append((ssem, dfree[par_d]))
                p.on("sync", lambda e, t=tc_, b=abuf2:
                     e.dma_start(out=b[:, :],
                                 in_=acc_s.ap()[t * TC:(t + 1) * TC, :]),
                     sem=dsem, inc=16, waits=w)
                p.on("sync", lambda e, t=tc_, b=rbuf:
                     e.dma_start(out=b[:, :],
                                 in_=residual_ap[t * TC:(t + 1) * TC, :]),
                     sem=dsem, inc=16)
                mdd = p.mark(dsem)
                p.on("vector", lambda e, t=tc_, b=abuf2:
                     e.scalar_tensor_tensor(big6[:, :], b[:, :],
                                            stat(80, t), hbo[:, :],
                                            ALU.mult, ALU.add),
                     waits=[(dsem, mdd), ("cV", m_hbo)])
                p.on("vector", lambda e, b=abuf2, r=rbuf:
                     e.tensor_add(b[:, :], big6[:, :], r[:, :]),
                     sem="cV", inc=1)
                dV[par_d] = p.mark("cV")
                mvd = p.mark("cV")
                p.on("gpsimd", lambda e, t=tc_, b=abuf2:
                     e.dma_start(out=dst_ap[t * TC:(t + 1) * TC, :],
                                 in_=b[:, :]),
                     sem=ssem, inc=16, waits=[("cV", mvd)])
                dfree[par_d] = p.mark(ssem)
            return (p.mark("dst"), p.mark("dst1"))

        if stages == "ffn1":
            m_dst = ffn(x_in, out_d, x_in, [], 0, True)
            p.on("gpsimd", lambda e: e.wait_ge(sems["dst"], m_dst[0]))
            p.on("gpsimd", lambda e: e.wait_ge(sems["dst1"], m_dst[1]))
            def emit(eng_name, engine):
                for waits, fn, sem, inc in p.steps[eng_name]:
                    for (s2, v) in waits:
                        engine.wait_ge(sems[s2], v)
                    inst = fn(engine)
                    if inc and inst is not None:
                        inst.then_inc(sems[sem], inc)
            @block.sync
            def _(sync):
                emit("sync", sync)
            @block.scalar
            def _(scalar):
                emit("scalar", scalar)
            @block.vector
            def _(vector):
                emit("vector", vector)
            @block.tensor
            def _(tensor):
                emit("tensor", tensor)
            @block.gpsimd
            def _(gpsimd):
                emit("gpsimd", gpsimd)
            return nc
        m_ffn1_dst = ffn(x_in, x2_s.ap(), x_in, [], 0, True)

        eV = {0: 0, 1: 0}
        eS = {0: 0, 1: 0}
        for tc_ in range(NTC):
            par_e = tc_ % 2
            ebuf = big0 if par_e == 0 else big1
            exsem = "ex" if par_e == 0 else "ex1"
            w = [("dst", m_ffn1_dst[0]), ("dst1", m_ffn1_dst[1])]
            if eV[par_e]:
                w += [("cV", eV[par_e]), ("cS", eS[par_e])]
            p.on("sync", lambda e, t=tc_, b=ebuf:
                 e.dma_start(out=b[:, :],
                             in_=x2_s.ap()[t * TC:(t + 1) * TC, :]),
                 sem=exsem, inc=16, waits=w)
            mdx = p.mark(exsem)
            p.on("vector", lambda e, b=ebuf:
                 e.reduce_sum(stats[:, 114:115], b[:, :], AX.X),
                 waits=[(exsem, mdx)])
            p.on("vector", lambda e:
                 e.tensor_single_scalar(stats[:, 114:115], stats[:, 114:115],
                                        1.0 / D, ALU.mult))
            p.on("scalar", lambda e, b=ebuf:
                 e.activation(big6[:, :], b[:, :], AF.Square,
                              bias=c_zero, scale=c_one,
                              accum_out=stats[:, 115:116]),
                 sem="cS", inc=1, waits=[(exsem, mdx), ("cV", p.mark("cV"))])
            msq = p.mark("cS")
            eS[par_e] = msq
            p.on("vector", lambda e:
                 e.tensor_single_scalar(stats[:, 118:119], stats[:, 115:116],
                                        1.0 / D, ALU.mult),
                 waits=[("cS", msq)])
            p.on("vector", lambda e:
                 e.tensor_mul(stats[:, 119:120], stats[:, 114:115],
                              stats[:, 114:115]))
            p.on("vector", lambda e:
                 e.tensor_sub(stats[:, 118:119], stats[:, 118:119],
                              stats[:, 119:120]), sem="cV", inc=1)
            mvv = p.mark("cV")
            p.on("scalar", lambda e:
                 e.activation(stats[:, 119:120], stats[:, 118:119], AF.Sqrt,
                              bias=c_eps, scale=c_one),
                 sem="cS", inc=1, waits=[("cV", mvv)])
            msr = p.mark("cS")
            p.on("vector", lambda e:
                 e.reciprocal(stats[:, 118:119], stats[:, 119:120]),
                 waits=[("cS", msr)])
            p.on("vector", lambda e, b=ebuf:
                 e.tensor_scalar(big5[:, :], b[:, :], stats[:, 114:115],
                                 stats[:, 118:119], ALU.subtract, ALU.mult),
                 sem="cV", inc=1, waits=[("cS", p.mark("cS"))])
            eV[par_e] = p.mark("cV")
            p.on("scalar", lambda e: e.copy(bigq[:, :], big5[:, :]),
                 sem="cS", inc=1,
                 waits=[("cT", p.mark("cT")), ("est", p.mark("est")),
                        ("cV", p.mark("cV"))])
            mzh = p.mark("cS")
            p.on("vector", lambda e:
                 e.tensor_tensor(bigq2[:, :], big5[:, :], bigq[:, :],
                                 ALU.subtract), sem="cV", inc=1,
                 waits=[("cS", mzh)])
            mz = p.mark("cV")
            p.on("gpsimd", lambda e, t=tc_:
                 e.dma_start(out=z2hl.ap()[t * TC:(t + 1) * TC, 0:D],
                             in_=bigq[:, :]), sem="est", inc=16,
                 waits=[("cS", mzh)])
            p.on("gpsimd", lambda e, t=tc_:
                 e.dma_start(out=z2hl.ap()[t * TC:(t + 1) * TC, D:2 * D],
                             in_=bigq2[:, :]), sem="est", inc=16,
                 waits=[("cV", mz)])
            for (qbuf, dstt) in [(bigq, z2t_h), (bigq2, z2t_l)]:
                for half in range(2):
                    ptb = pt[half]
                    for kk in range(8):
                        jc = half * 8 + kk
                        p.on("tensor", lambda e, j=jc, k=kk, b=ptb, q=qbuf:
                             e.transpose(b[:, k * 128:(k + 1) * 128],
                                         q[:, j * 128:(j + 1) * 128],
                                         ident[:, :]),
                             sem="cT" if kk == 7 else None,
                             inc=1 if kk == 7 else 0,
                             waits=[("cV", max(mz, p.mark("cV"))),
                                    ("cS", p.mark("cS"))]
                             if kk == 0 else [])
                    mtt = p.mark("cT")
                    stg = wch0 if half == 0 else wch1
                    if half == 0:
                        p.on("scalar", lambda e, b=ptb, s=stg:
                             e.copy(s[:, 0:1024], b[:, :]),
                             sem="cS", inc=1,
                             waits=[("cT", mtt), ("est", p.mark("est"))])
                        mvt = ("cS", p.mark("cS"))
                    else:
                        p.on("vector", lambda e, b=ptb, s=stg:
                             e.tensor_copy(s[:, 0:1024], b[:, :]),
                             sem="cV", inc=1,
                             waits=[("cT", mtt), ("est", p.mark("est"))])
                        mvt = ("cV", p.mark("cV"))

                    def st_t(e, t=tc_, h=half, s=stg, d=dstt):
                        dview = d.ap()[h * 1024:(h + 1) * 1024,
                                       t * 128:(t + 1) * 128]
                        dr = dview.rearrange("(k p) c -> p k c", p=128)
                        sr = s[:, 0:1024].rearrange("p (k c) -> p k c", c=128)
                        return e.dma_start(out=dr, in_=sr)
                    p.on("gpsimd", st_t, sem="est", inc=16,
                         waits=[mvt])
        m_est = p.mark("est")

        def do_ag(e):
            e.collective_compute(
                "AllGather", ALU.bypass,
                replica_groups=[[0, 1], [2, 3], [4, 5], [6, 7]],
                ins=[z2hl.ap().opt()],
                outs=[z2full.ap().opt()]).then_inc(sems["cc"], 1)
            return None
        p.on("gpsimd", do_ag, waits=[("est", m_est)])

        for jbi in range(NJB):
            wv = [("cT", p.mark("cT"))]
            for (hl, base) in [(0, 0), (1, D)]:
                for cc2 in range(NCC):
                    p.on("sync", lambda e, j=jbi, c=cc2, b=base, o=hl:
                         e.dma_start(
                             out=qt[:, (o * NCC + c) * 512:
                                    (o * NCC + c + 1) * 512],
                             in_=z2full.ap()[c * 128:(c + 1) * 128,
                                             b + j * 512:b + (j + 1) * 512]),
                         sem="jb", inc=16,
                         waits=(wv + [("cc", 1)])
                         if (cc2 == 0 and hl == 0) else [])
            m_jb = p.mark("jb")
            for ic in range(16):
                ti = jbi * 16 + ic
                ibh = wch0 if ic % 2 == 0 else wch1
                zsem = "zi0" if ic % 2 == 0 else "zi1"
                wv2 = [("cT", p.mark("cT"))]
                for (hl, base) in [(0, 0), (1, D)]:
                    def ld_i(e, i=ic, b=base, buf=ibh, o=hl):
                        src = z2full.ap()[0:CT, b + i * 128:b + (i + 1) * 128]
                        sr = src.rearrange("(c p) i -> p c i", p=128)
                        dst2 = buf[:, o * NCC * 128:(o + 1) * NCC * 128]
                        dr = dst2.rearrange("p (c i) -> p c i", i=128)
                        return e.dma_start(out=dr, in_=sr)
                    p.on("sync", ld_i, sem=zsem, inc=16,
                         waits=([("jb", m_jb)] + wv2) if hl == 0 else [])
                m_ic = p.mark(zsem)
                par = ti % 2
                pg = pb[par]
                gsem = "gst0" if par == 0 else "gst1"
                wv3 = [("cV", p.mark("cV"))] if ti >= 2 else []
                passes = [(c3, la, ra) for c3 in range(NCC)
                          for (la, ra) in [(0, 0), (0, 1), (1, 0)]]
                for idx, (c3, la, ra) in enumerate(passes):
                    last = idx == len(passes) - 1
                    p.on("tensor", lambda e, lo=(la * NCC + c3) * 128,
                         ro=(ra * NCC + c3) * 512, g=pg, b=ibh,
                         st=(idx == 0), sp=last:
                         e.matmul(g[:, 0:512], b[:, lo:lo + 128],
                                  qt[:, ro:ro + 512], start=st, stop=sp),
                         sem="cT" if last else None, inc=1 if last else 0,
                         waits=([(zsem, m_ic), ("jb", m_jb)] + wv3)
                         if idx == 0 else [])
                mt = p.mark("cT")
                stg = ev["ost0"] if par == 0 else ev["ost1"]
                wg = [(gsem, 16 * (ti // 2))] if ti >= 2 else []
                p.on("vector", lambda e, g=pg, s=stg:
                     e.tensor_copy(s[:, :], g[:, 0:512]), sem="cV", inc=1,
                     waits=[("cT", mt)] + wg)
                mve = p.mark("cV")
                p.on("gpsimd", lambda e, i=ic, j=jbi, s=stg:
                     e.dma_start(out=g_s.ap()[i * 128:(i + 1) * 128,
                                              j * 512:(j + 1) * 512],
                                 in_=s[:, :]),
                     sem=gsem, inc=16, waits=[("cV", mve)])
        m_gst = [p.mark("gst0"), p.mark("gst1")]
        m_gram_t = p.mark("cT")

        gV = {0: 0, 1: 0}
        gS = {0: 0, 1: 0}
        for ic in range(16):
            par_g = ic % 2
            gbuf = big0 if par_g == 0 else big5
            glsem = "gld" if par_g == 0 else "gld1"
            w = [("gst0", m_gst[0]), ("gst1", m_gst[1])]
            if gV[par_g]:
                w += [("cV", gV[par_g]), ("cS", gS[par_g])]
            p.on("sync", lambda e, i=ic, b=gbuf:
                 e.dma_start(out=b[:, :],
                             in_=g_s.ap()[i * 128:(i + 1) * 128, :]),
                 sem=glsem, inc=16, waits=w)
            mdg = p.mark(glsem)
            p.on("vector", lambda e, b=gbuf:
                 e.reduce_max(stats[:, 114:115], b[:, :], AX.X),
                 waits=[(glsem, mdg)])
            gV[par_g] = p.mark("cV")
            p.on("vector", lambda e:
                 e.tensor_single_scalar(stats[:, 115:116], stats[:, 114:115],
                                        -1.0, ALU.mult), sem="cV", inc=1)
            mvm = p.mark("cV")
            p.on("scalar", lambda e, b=gbuf:
                 e.activation(big1[:, :], b[:, :], AF.Exp,
                              bias=stats[:, 115:116], scale=c_one,
                              accum_out=stats[:, 116:117]),
                 sem="cS", inc=1, waits=[("cV", mvm)])
            mse = p.mark("cS")
            gS[par_g] = mse
            p.on("vector", lambda e:
                 e.reciprocal(stats[:, 117:118], stats[:, 116:117]),
                 waits=[("cS", mse)])
            p.on("vector", lambda e:
                 e.tensor_single_scalar(big6[:, :], big1[:, :],
                                        stats[:, 117:118], ALU.mult),
                 waits=[("cS", p.mark("cS"))])
            p.on("scalar", lambda e: e.copy(bigq[:, :], big6[:, :]),
                 sem="cS", inc=1,
                 waits=[("ast", p.mark("ast")), ("cV", p.mark("cV"))])
            mha = p.mark("cS")
            p.on("vector", lambda e:
                 e.tensor_tensor(bigq2[:, :], big6[:, :], bigq[:, :],
                                 ALU.subtract), sem="cV", inc=1,
                 waits=[("cS", mha)])
            mva = p.mark("cV")
            p.on("gpsimd", lambda e, i=ic:
                 e.dma_start(out=ah_s.ap()[i * 128:(i + 1) * 128, :],
                             in_=bigq[:, :]), sem="ast", inc=16,
                 waits=[("cS", mha)])
            p.on("gpsimd", lambda e, i=ic:
                 e.dma_start(out=al_s.ap()[i * 128:(i + 1) * 128, :],
                             in_=bigq2[:, :]), sem="ast", inc=16,
                 waits=[("cV", mva)])
        m_ast = p.mark("ast")

        ctx_add_mark = {}
        for ib in range(NIB):
            wv = [("cT", p.mark("cT"))]
            for (hl, src) in [(0, ah_s), (1, al_s)]:
                for jc in range(16):
                    p.on("scalar", lambda e, i=ib, j=jc, s=src, o=hl:
                         e.dma_start(
                             out=qt[:, (o * 16 + j) * 512:
                                    (o * 16 + j + 1) * 512],
                             in_=s.ap()[i * 512:(i + 1) * 512,
                                        j * 128:(j + 1) * 128],
                             transpose=True),
                         sem="wld", inc=16,
                         waits=([("ast", m_ast)] + wv)
                         if (jc == 0 and hl == 0) else [])
            m_at = p.mark("wld")
            for cc2 in range(NTC):
                ti = ib * NTC + cc2
                parb = cc2 % 2
                zb = wch0 if parb == 0 else wch1
                ztsem = "zt0" if parb == 0 else "zt1"
                xsem = "x2l0" if parb == 0 else "x2l1"
                wv2 = [("cT", p.mark("cT"))] if ti >= 2 else \
                    [("cT", m_gram_t)]
                for (hl, src) in [(0, z2t_h), (1, z2t_l)]:
                    def ld_zt(e, c=cc2, s=src, o=hl, buf=zb):
                        sv = s.ap()[0:D, c * 128:(c + 1) * 128]
                        sr = sv.rearrange("(j p) c -> p j c", p=128)
                        dst2 = buf[:, o * 16 * 128:(o + 1) * 16 * 128]
                        dr = dst2.rearrange("p (j c) -> p j c", c=128)
                        return e.dma_start(out=dr, in_=sr)
                    p.on("sync", ld_zt, sem=ztsem, inc=16,
                         waits=([("est", m_est)] + wv2) if hl == 0 else [])
                m_zt = p.mark(ztsem)
                x2t = ev["x2a"] if parb == 0 else ev["x2b"]
                wx = [("cV", ctx_add_mark[ti - 2])] if ti >= 2 else []
                p.on("sync", lambda e, c=cc2, i=ib, o=x2t:
                     e.dma_start(out=o[:, :],
                                 in_=x2_s.ap()[c * 128:(c + 1) * 128,
                                               i * 512:(i + 1) * 512]),
                     sem=xsem, inc=16, waits=wx)
                m_x2 = p.mark(xsem)
                par = ti % 2
                po = pb[2 + par]
                xssem = "xst0" if par == 0 else "xst1"
                wv3 = [("cV", p.mark("cV"))] if ti >= 2 else []
                passes = [(jc, la, ra) for jc in range(16)
                          for (la, ra) in [(0, 0), (0, 1), (1, 0)]]
                for idx, (jc, la, ra) in enumerate(passes):
                    last = idx == len(passes) - 1
                    p.on("tensor", lambda e, lo=(la * 16 + jc) * 128,
                         ro=(ra * 16 + jc) * 512, o=po, b=zb,
                         st=(idx == 0), sp=last:
                         e.matmul(o[:, 0:512], b[:, lo:lo + 128],
                                  qt[:, ro:ro + 512], start=st, stop=sp),
                         sem="cT" if last else None, inc=1 if last else 0,
                         waits=([(ztsem, m_zt), ("wld", m_at)] + wv3)
                         if idx == 0 else [])
                mt = p.mark("cT")
                stg = ev["ost0"] if par == 0 else ev["ost1"]
                wst = [(xssem, 16 * (ti // 2))] if ti >= 2 else []
                p.on("vector", lambda e, o=po, s=stg, xx=x2t:
                     e.tensor_add(s[:, :], o[:, 0:512], xx[:, :]),
                     sem="cV", inc=1,
                     waits=[("cT", mt), (xsem, m_x2)] + wst)
                ctx_add_mark[ti] = p.mark("cV")
                mve = p.mark("cV")
                p.on("gpsimd", lambda e, c=cc2, i=ib, s=stg:
                     e.dma_start(out=x3_s.ap()[c * 128:(c + 1) * 128,
                                               i * 512:(i + 1) * 512],
                                 in_=s[:, :]),
                     sem=xssem, inc=16, waits=[("cV", mve)])
        m_xst = [p.mark("xst0"), p.mark("xst1")]
        m_ctx_t = p.mark("cT")

        m_ffn3_dst = ffn(x3_s.ap(), out_d, x3_s.ap(),
                         [("xst0", m_xst[0]), ("xst1", m_xst[1])],
                         m_ctx_t, False)

        p.on("gpsimd", lambda e: e.wait_ge(sems["dst"], m_ffn3_dst[0]))
        p.on("gpsimd", lambda e: e.wait_ge(sems["dst1"], m_ffn3_dst[1]))

        def emit(eng_name, engine):
            for waits, fn, sem, inc in p.steps[eng_name]:
                for (s2, v) in waits:
                    engine.wait_ge(sems[s2], v)
                inst = fn(engine)
                if inc and inst is not None:
                    inst.then_inc(sems[sem], inc)

        @block.sync
        def _(sync):
            emit("sync", sync)

        @block.scalar
        def _(scalar):
            emit("scalar", scalar)

        @block.vector
        def _(vector):
            emit("vector", vector)

        @block.tensor
        def _(tensor):
            emit("tensor", tensor)

        @block.gpsimd
        def _(gpsimd):
            emit("gpsimd", gpsimd)

    return nc


_NC_CACHE = None


def kernel(L0_attn_x, L0_ffn_gate_w, L0_ffn_gate_b, L0_ffn_val_w,
           L0_ffn_val_b, L0_ffn_out_w, L0_ffn_out_b):
    global _NC_CACHE
    x = np.ascontiguousarray(np.asarray(L0_attn_x, dtype=np.float32))
    gw_a = np.ascontiguousarray(np.asarray(L0_ffn_gate_w, dtype=np.float32))
    vw_a = np.ascontiguousarray(np.asarray(L0_ffn_val_w, dtype=np.float32))
    ow_a = np.ascontiguousarray(np.asarray(L0_ffn_out_w, dtype=np.float32))
    bg_a = np.ascontiguousarray(
        np.asarray(L0_ffn_gate_b, dtype=np.float32).reshape(1, HID))
    bv_a = np.ascontiguousarray(
        np.asarray(L0_ffn_val_b, dtype=np.float32).reshape(1, HID))
    bo_a = np.ascontiguousarray(
        np.asarray(L0_ffn_out_b, dtype=np.float32).reshape(1, D))
    ident = np.eye(128, dtype=ml_dtypes.bfloat16)
    cstrow = np.array([0.0, 1.0, 1e-5, 0.5, -0.5, 1.0 / D, 127.0, 0.0],
                      np.float32)
    cst_a = np.ascontiguousarray(np.broadcast_to(cstrow, (128, 8)))

    if _NC_CACHE is None:
        _NC_CACHE = build_program()
    nc = _NC_CACHE

    in_maps = []
    for c in range(8):
        b, hh = divmod(c, 2)
        in_maps.append({
            "x": np.ascontiguousarray(x[b, hh * T:(hh + 1) * T, :]),
            "gw": gw_a, "vw": vw_a, "ow": ow_a,
            "bg": bg_a, "bv": bv_a, "bo": bo_a,
            "ident": ident, "cst": cst_a,
        })
    res = run_bass_kernel_spmd(nc, in_maps, core_ids=list(range(8)))
    out = np.empty((4, 4096, D), np.float32)
    for c in range(8):
        b, hh = divmod(c, 2)
        out[b, hh * T:(hh + 1) * T, :] = np.asarray(res.results[c]["out"])
    return out
